# version 01
# baseline (speedup 1.0000x reference)
import contextlib
import numpy as np
import ml_dtypes
import concourse.bass as bass
import concourse.mybir as mybir
from concourse.bass_utils import run_bass_kernel_spmd

F32 = mybir.dt.float32
BF16 = mybir.dt.bfloat16
AF = mybir.ActivationFunctionType
ALU = mybir.AluOpType
AX = mybir.AxisListType
BF = ml_dtypes.bfloat16

MAGIC = 12582912.0
EPS = 1e-5
L = 128
DM = 1024
DI = 2048
NS = 16
DF = 4096


class KB:
    ENGMAP = {
        'pe':   ('tensor', 'pe', 1),
        'dve':  ('vector', 'dve', 1),
        'act':  ('scalar', 'act', 1),
        'pool': ('gpsimd', 'pool', 1),
        'qs':   ('sync',   'qs', 16),
        'qg':   ('gpsimd', 'qg', 16),
        'cc':   ('gpsimd', 'cc', 1),
    }
    STREAMS = ('tensor', 'vector', 'scalar', 'gpsimd', 'sync')

    def __init__(self, nc, sems):
        self.nc = nc
        self.sems = sems
        self.cnt = {k: 0 for k in sems}
        self.ops = {s: [] for s in self.STREAMS}
        self.waited = {}
        self.lastw = {}
        self.readers = {}

    def _wait(self, stream, sem, val):
        key = (stream, sem)
        if val <= self.waited.get(key, 0):
            return
        self.waited[key] = val
        h = self.sems[sem]
        self.ops[stream].append(lambda e, h=h, v=val: e.wait_ge(h, v))

    def op(self, eng, fn, reads=(), writes=()):
        stream, sem, inc = self.ENGMAP[eng]
        deps = {}

        def add(tok):
            if tok is None:
                return
            s2, c2 = tok
            if eng == 'pe' and s2 == 'pe':
                return
            if c2 > deps.get(s2, 0):
                deps[s2] = c2

        for r in reads:
            add(self.lastw.get(r))
        for w in writes:
            add(self.lastw.get(w))
            for item in self.readers.get(w, {}).items():
                add(item)
        if inc == 16:
            self._wait(stream, sem, self.cnt[sem])
        for s2, c2 in sorted(deps.items()):
            self._wait(stream, s2, c2)
        self.cnt[sem] += inc
        mycnt = self.cnt[sem]
        h = self.sems[sem]

        def emit(e, fn=fn, h=h, inc=inc):
            inst = fn(e)
            inst.then_inc(h, inc)

        self.ops[stream].append(emit)
        tok = (sem, mycnt)
        for r in reads:
            self.readers.setdefault(r, {})[sem] = mycnt
        for w in writes:
            self.lastw[w] = tok
            self.readers[w] = {}
        return tok

    def barrier(self):
        snap = [(s, c) for s, c in self.cnt.items() if c > 0]
        for stream in self.STREAMS:
            for sem, c in snap:
                self._wait(stream, sem, c)

    def flush(self, blk):
        for stream in self.STREAMS:
            ops = self.ops[stream]
            if not ops:
                continue
            section = getattr(blk, {'tensor': 'tensor', 'vector': 'vector',
                                    'scalar': 'scalar', 'gpsimd': 'gpsimd',
                                    'sync': 'sync'}[stream])

            def body(e, ops=ops):
                for f in ops:
                    f(e)
            section(body)


def build_program(nch=32):
    T = nch * L
    NE = nch + 2
    nc = bass.Bass("TRN2", target_bir_lowering=False)

    def din(name, shape, dt=F32):
        return nc.dram_tensor(name, shape, dt, kind="ExternalInput").ap()

    def dint(name, shape, dt):
        return nc.dram_tensor(name, shape, dt, kind="Internal")

    x_in = din("x_in", [T, DM])
    xl = din("xl", [L, DM]); xr = din("xr", [L, DM])
    w_inT = din("w_inT", [DM, 2 * DI])
    in_b = din("in_b", [1, 2 * DI])
    in_w_xc = din("in_w_xc", [DI, DM])
    dt_wT = din("dt_wT", [DI, NS])
    dt_b_row = din("dt_b_row", [1, NS])
    xpw_kc = din("xpw_kc", [128, NS])
    xpb20 = din("xpb20", [1, 1])
    convw_kc = din("convw_kc", [128, NS * 4])
    convb_kc = din("convb_kc", [128, NS])
    a16 = din("a16", [NS, 1])
    bmT = din("bmT", [NS, DI]); cmT = din("cmT", [NS, DI])
    bm_rows = din("bm_rows", [NS, DI])
    cm_rows = din("cm_rows", [NS, DI])
    out_wT = din("out_wT", [DI, DM])
    out_b = din("out_b", [1, DM])
    wgT = din("wgT", [DM, DF]); wuT = din("wuT", [DM, DF]); wdT = din("wdT", [DF, DM])
    selmat = din("selmat", [128, NS])
    ident_bf = din("ident_bf", [128, 128], BF16)
    ident_f = din("ident_f", [128, 128], F32)
    c_rgts = din("c_rgts", [128, 128], BF16)
    c_rlet = din("c_rlet", [128, 128], BF16)
    c_mnegt = din("c_mnegt", [128, 128], BF16)

    out_h = nc.dram_tensor("out", [T, DM], F32, kind="ExternalOutput").ap()

    wres_h = dint("wres_h", [8, 128, DI], BF16).ap()
    wgq_h = dint("wgq_h", [8, 128, DF], BF16).ap()
    wuq_h = dint("wuq_h", [8, 128, DF], BF16).ap()
    wdq_h = dint("wdq_h", [32, 128, DM], BF16).ap()
    res_h = dint("res_h", [nch, L, DI], BF16).ap()
    yloc_h = dint("yloc_h", [nch, L, DI], BF16).ap()
    x2_h = dint("x2_h", [nch, L, DM], F32).ap()
    q2_h = dint("q2_h", [nch, 128, DF], BF16).ap()
    pl_h = dint("pl_h", [nch, NS, L], BF16).ap()
    ag_in_t = dint("ag_in", [NS, DI], F32)
    ag_out_t = dint("ag_out", [8 * NS, DI], F32)
    ag_in = ag_in_t.ap(); ag_out = ag_out_t.ap()

    sem_names = ['pe', 'dve', 'act', 'pool', 'qs', 'qg', 'cc']
    PSD_ALL = ['psd', 'psdd0', 'psdd1', 'psdd2', 'psdd3']

    with contextlib.ExitStack() as S0:
        sems = {n: S0.enter_context(nc.semaphore(n)) for n in sem_names}
        K = KB(nc, sems)

        ps_a = S0.enter_context(nc.psum_tensor("ps_a", [128, 512], F32))
        ps_b = S0.enter_context(nc.psum_tensor("ps_b", [128, 512], F32))
        ps_y = S0.enter_context(nc.psum_tensor("ps_y", [128, 2048], F32))
        ps_d = S0.enter_context(nc.psum_tensor("ps_d", [128, 512], F32))
        ps_t = S0.enter_context(nc.psum_tensor("ps_t", [128, 1024], BF16))

        sb = lambda n, sh, dt=F32: S0.enter_context(nc.sbuf_tensor(n, sh, dt))
        ones_row = sb("ones_row", [1, 128], BF16)
        ones_row_f = sb("ones_row_f", [1, 128])
        one11 = sb("one11", [1, 1], BF16)
        idb = sb("idb", [128, 128], BF16)
        idf = sb("idf", [128, 128], F32)
        rgts = sb("rgts", [128, 128], BF16)
        rlet = sb("rlet", [128, 128], BF16)
        mnegt = sb("mnegt", [128, 128], BF16)
        gcols = sb("gcols", [128, nch])
        ds2_all = sb("ds2_all", [128, nch], F32)
        cgcol = sb("cgcol", [128, 1]); cucol = sb("cucol", [128, 1])
        cdcol = sb("cdcol", [128, 1]); ngcol = sb("ngcol", [128, 1])
        h0c = sb("h0c", [NS, DI], BF16)
        spa_col = sb("spa_col", [NS, 1])
        dtcb = sb("dtcb", [NS, 1])
        gamc = sb("gamc", [1, 1])
        sQ = {}
        for wn in ('g', 'u', 'd'):
            sQ[wn] = sb(f"sw_{wn}", [1, 1])
            sQ[wn + 'col'] = sb(f"swc_{wn}", [128, 1])
            sQ[wn + 'c'] = sb(f"cw_{wn}", [1, 1])

        K.op('qs', lambda e: e.dma_start(out=idb[:], in_=ident_bf[:]), writes=['idb'])
        K.op('qs', lambda e: e.dma_start(out=idf[:], in_=ident_f[:]), writes=['idf'])
        K.op('qs', lambda e: e.dma_start(out=rgts[:], in_=c_rgts[:]), writes=['rgts'])
        K.op('qs', lambda e: e.dma_start(out=rlet[:], in_=c_rlet[:]), writes=['rlet'])
        K.op('qs', lambda e: e.dma_start(out=mnegt[:], in_=c_mnegt[:]), writes=['mnegt'])
        K.op('dve', lambda e: e.memset(ones_row[:], 1.0), writes=['ones_row'])
        K.op('dve', lambda e: e.memset(ones_row_f[:], 1.0), writes=['ones_row_f'])
        K.op('dve', lambda e: e.memset(one11[:], 1.0), writes=['one11'])

        def sp_chain(dst, src, region_src, region_dst, scratch, rs, P):
            t1, t2 = scratch
            r1, r2 = rs
            K.op('act', lambda e: e.activation(t1[:P, :], src, AF.Abs),
                 reads=[region_src], writes=[r1])
            K.op('act', lambda e: e.activation(t1[:P, :], t1[:P, :], AF.Exp, scale=np.float32(-1.0)),
                 reads=[r1], writes=[r1])
            K.op('act', lambda e: e.activation(t1[:P, :], t1[:P, :], AF.Ln, bias=np.float32(1.0)),
                 reads=[r1], writes=[r1])
            K.op('dve', lambda e: e.tensor_scalar(t2[:P, :], src, 0.0, None, ALU.max, ALU.bypass),
                 reads=[region_src], writes=[r2])
            K.op('dve', lambda e: e.tensor_tensor(dst, t1[:P, :], t2[:P, :], ALU.add),
                 reads=[r1, r2], writes=[region_dst])

        with contextlib.ExitStack() as SA:
            sa = lambda n, sh, dt=F32: SA.enter_context(nc.sbuf_tensor(n, sh, dt))
            wxc_bf = sa("wxc_bf", [128, 8 * DI], BF16)
            ebc = sa("ebc", [128, NS * DI], BF16)
            dtw_all = sa("dtw_all", [128, 4 * 8 * NS], BF16)
            xpw_all = sa("xpw_all", [128, 4 * 8], BF16)
            dbt = sa("dbt", [NS, DI])
            cmt_s = sa("cmt_s", [NS, DI])
            hstate = sa("hstate", [NS, DI])
            lam = sa("lam", [NS, 1])
            ones16 = sa("ones16", [NS, 128])
            hc = sa("hc", [NS, DI], BF16)
            inb_bf = sa("inb_bf", [1, 2 * DI], BF16)
            dtb_bf = sa("dtb_bf", [1, NS], BF16)
            xpb_bf = sa("xpb_bf", [1, 1], BF16)

            K.op('dve', lambda e: e.memset(hstate[:], 0.0), writes=['H'])
            K.op('dve', lambda e: e.memset(lam[:], 1.0), writes=['lam'])
            K.op('dve', lambda e: e.memset(ones16[:], 1.0), writes=['ones16'])

            with contextlib.ExitStack() as SP:
                sp = lambda n, sh, dt=F32: SP.enter_context(nc.sbuf_tensor(n, sh, dt))
                stg = [sp("stg0", [128, DI]), sp("stg1", [128, DI])]
                stg_b = sp("stg_b", [128, DI], BF16)
                inwxc_bf = [sp("inwxc0", [128, DM], BF16), sp("inwxc1", [128, DM], BF16)]
                row_a = sp("row_a", [1, 2 * DI])
                dtwt_bf = sp("dtwt_bf", [128, 16 * NS], BF16)
                cwf = sp("cwf", [128, NS * 4]); cbf = sp("cbf", [128, NS])
                cb_bf = sp("cb_bf", [128, NS], BF16)
                xpw_f = sp("xpw_f", [128, NS]); xpw_bf2 = sp("xpw_bf2", [128, NS], BF16)
                rk_bf = sp("rk_bf", [128, NS], BF16)
                rk1_bf = sp("rk1_bf", [128, 1], BF16)
                acol = sp("acol", [NS, 1]); acs = [sp("acs0", [NS, 1]), sp("acs1", [NS, 1])]

                for kc in range(8):
                    s = stg[kc % 2]; rs = f'stg{kc % 2}'
                    K.op('qs', lambda e, kc=kc, s=s: e.dma_start(
                        out=s[:], in_=w_inT[kc * 128:(kc + 1) * 128, 0:DI]), writes=[rs])
                    K.op('act', lambda e, kc=kc, s=s: e.activation(
                        wxc_bf[:, kc * DI:(kc + 1) * DI], s[:], AF.Copy),
                        reads=[rs], writes=['wxc_bf'])
                    K.op('qs', lambda e, kc=kc, s=s: e.dma_start(
                        out=s[:], in_=w_inT[kc * 128:(kc + 1) * 128, DI:2 * DI]), writes=[rs])
                    K.op('act', lambda e, kc=kc, s=s: e.activation(
                        stg_b[:], s[:], AF.Copy),
                        reads=[rs], writes=['stg_b'])
                    K.op('qg', lambda e, kc=kc: e.dma_start(
                        out=wres_h[kc], in_=stg_b[:]), reads=['stg_b'], writes=['wres_h'])
                K.op('qs', lambda e: e.dma_start(out=row_a[:, 0:2 * DI], in_=in_b[:]),
                     writes=['rowsc'])
                K.op('act', lambda e: e.activation(inb_bf[:], row_a[:, 0:2 * DI], AF.Copy),
                     reads=['rowsc'], writes=['inb_bf'])
                K.op('qs', lambda e: e.dma_start(out=row_a[:, 0:NS], in_=dt_b_row[:]),
                     writes=['rowsc'])
                K.op('act', lambda e: e.activation(dtb_bf[:], row_a[:, 0:NS], AF.Copy),
                     reads=['rowsc'], writes=['dtb_bf'])
                K.op('qs', lambda e: e.dma_start(out=row_a[:, 0:1], in_=xpb20[:]),
                     writes=['rowsc'])
                K.op('act', lambda e: e.activation(xpb_bf[:], row_a[:, 0:1], AF.Copy),
                     reads=['rowsc'], writes=['xpb_bf'])

                for kc in range(16):
                    s = stg[kc % 2]; rs = f'stg{kc % 2}'
                    K.op('qs', lambda e, kc=kc, s=s: e.dma_start(
                        out=s[:, 0:NS], in_=dt_wT[kc * 128:(kc + 1) * 128, :]), writes=[rs])
                    K.op('act', lambda e, kc=kc, s=s: e.activation(
                        dtwt_bf[:, kc * NS:(kc + 1) * NS], s[:, 0:NS], AF.Copy),
                        reads=[rs], writes=['dtwt_bf'])
                K.op('qs', lambda e: e.dma_start(out=cwf[:], in_=convw_kc[:]), writes=['cwf'])
                K.op('qs', lambda e: e.dma_start(out=cbf[:], in_=convb_kc[:]), writes=['cbf'])
                K.op('act', lambda e: e.activation(cb_bf[:], cbf[:], AF.Copy),
                     reads=['cbf'], writes=['cb_bf'])
                K.op('qs', lambda e: e.dma_start(out=xpw_f[:], in_=xpw_kc[:]), writes=['xpw_f'])
                K.op('act', lambda e: e.activation(xpw_bf2[:], xpw_f[:], AF.Copy),
                     reads=['xpw_f'], writes=['xpw_bf2'])

                for kc in range(16):
                    w = inwxc_bf[kc % 2]; wr = f'inwxc{kc % 2}'
                    s = stg[kc % 2]; rs = f'stg{kc % 2}'
                    K.op('qs', lambda e, kc=kc, s=s: e.dma_start(
                        out=s[:, 0:DM], in_=in_w_xc[kc * 128:(kc + 1) * 128, :]), writes=[rs])
                    K.op('act', lambda e, s=s, w=w: e.activation(
                        w[:], s[:, 0:DM], AF.Copy), reads=[rs], writes=[wr])
                    for k in range(4):
                        K.op('dve', lambda e, k=k, kc=kc: e.tensor_scalar(
                            rk_bf[:], dtwt_bf[:, kc * NS:(kc + 1) * NS],
                            cwf[:, kc * 4 + k:kc * 4 + k + 1], None, ALU.mult, ALU.bypass),
                            reads=['dtwt_bf', 'cwf'], writes=['rk_bf'])
                        K.op('dve', lambda e, k=k, kc=kc: e.tensor_scalar(
                            rk1_bf[:], xpw_bf2[:, kc:kc + 1],
                            cwf[:, kc * 4 + k:kc * 4 + k + 1], None, ALU.mult, ALU.bypass),
                            reads=['xpw_bf2', 'cwf'], writes=['rk1_bf'])

                        first = (kc == 0 and k == 0)
                        last = (kc == 15 and k == 3)

                        def dtwmm(e, k=k, kc=kc, w=w, first=first, last=last):
                            inst = None
                            for ic in range(8):
                                inst = e.matmul(
                                    ps_b[:, k * 128 + ic * NS: k * 128 + (ic + 1) * NS],
                                    w[:, ic * 128:(ic + 1) * 128],
                                    rk_bf[:], start=(first and ic == 0),
                                    stop=(last and ic == 7))
                            return inst
                        K.op('pe', dtwmm, reads=[wr, 'rk_bf'], writes=['psb'])

                        def xpwmm(e, k=k, kc=kc, w=w, first=first, last=last):
                            inst = None
                            for ic in range(8):
                                inst = e.matmul(
                                    ps_a[:, k * 8 + ic: k * 8 + ic + 1],
                                    w[:, ic * 128:(ic + 1) * 128],
                                    rk1_bf[:], start=(first and ic == 0),
                                    stop=(last and ic == 7))
                            return inst
                        K.op('pe', xpwmm, reads=[wr, 'rk1_bf'], writes=['psa'])
                K.op('act', lambda e: e.activation(
                    dtw_all[:], ps_b[:, 0:512], AF.Copy),
                    reads=['psb'], writes=['dtw_all'])
                K.op('act', lambda e: e.activation(
                    xpw_all[:], ps_a[:, 0:32], AF.Copy),
                    reads=['psa'], writes=['xpw_all'])

                for kc in range(16):
                    K.op('pe', lambda e, kc=kc: e.matmul(
                        ps_d[0:NS, 0:1], dtwt_bf[:, kc * NS:(kc + 1) * NS],
                        cb_bf[:, kc:kc + 1], start=(kc == 0), stop=False),
                        reads=['dtwt_bf', 'cb_bf'], writes=['psdd0'])
                    K.op('pe', lambda e, kc=kc: e.matmul(
                        ps_a[0:1, 0:1], xpw_bf2[:, kc:kc + 1],
                        cb_bf[:, kc:kc + 1], start=(kc == 0), stop=False),
                        reads=['xpw_bf2', 'cb_bf'], writes=['psa'])
                K.op('pe', lambda e: e.matmul(ps_d[0:NS, 0:1], dtb_bf[:],
                                              one11[:], start=False, stop=True),
                     reads=['dtb_bf', 'one11'], writes=['psdd0'])
                K.op('pe', lambda e: e.matmul(ps_a[0:1, 0:1], xpb_bf[:],
                                              one11[:], start=False, stop=True),
                     reads=['xpb_bf', 'one11'], writes=['psa'])
                K.op('dve', lambda e: e.tensor_copy(dtcb[:], ps_d[0:NS, 0:1]),
                     reads=['psdd0'], writes=['dtcb'])
                K.op('dve', lambda e: e.tensor_copy(gamc[:], ps_a[0:1, 0:1]),
                     reads=['psa'], writes=['gamc'])

                K.op('qs', lambda e: e.dma_start(out=acol[:], in_=a16[:]), writes=['acol'])
                sp_chain(spa_col[:], acol[:], 'acol', 'spa_col',
                         (acs[0], acs[1]), ('acs0', 'acs1'), NS)

                K.op('qs', lambda e: e.dma_start(out=dbt[:], in_=bmT[:]), writes=['dbt_raw'])
                K.op('act', lambda e: e.activation(dbt[:], dbt[:], AF.Abs),
                     reads=['dbt_raw'], writes=['dbt_abs'])
                K.op('qs', lambda e: e.dma_start(out=cmt_s[:], in_=cmT[:]), writes=['cmt'])

                dbt2 = stg[0]
                K.op('qs', lambda e: e.dma_start(out=dbt2[0:NS, :], in_=bmT[:]), writes=['stg0'])
                K.op('act', lambda e: e.activation(dbt[:], dbt[:], AF.Exp, scale=np.float32(-1.0)),
                     reads=['dbt_abs'], writes=['dbt_abs'])
                K.op('act', lambda e: e.activation(dbt[:], dbt[:], AF.Ln, bias=np.float32(1.0)),
                     reads=['dbt_abs'], writes=['dbt_abs'])
                K.op('dve', lambda e: e.tensor_scalar(dbt2[0:NS, :], dbt2[0:NS, :], 0.0, None,
                                                      ALU.max, ALU.bypass),
                     reads=['stg0'], writes=['stg0'])
                K.op('dve', lambda e: e.tensor_tensor(dbt[:], dbt[:], dbt2[0:NS, :], ALU.add),
                     reads=['dbt_abs', 'stg0'], writes=['dbt'])

                erow = row_a[0:1, 0:DI]
                erow2 = row_a[0:1, DI:2 * DI]
                erow_bf = sp("erow_bf", [1, DI], BF16)
                for n in range(NS):
                    K.op('qs', lambda e, n=n: e.dma_start(
                        out=erow, in_=bm_rows[n:n + 1, :]), writes=['rowsc'])
                    K.op('act', lambda e: e.activation(erow2, erow, AF.Abs),
                         reads=['rowsc'], writes=['rowsc'])
                    K.op('act', lambda e: e.activation(erow2, erow2, AF.Exp, scale=np.float32(-1.0)),
                         reads=['rowsc'], writes=['rowsc'])
                    K.op('act', lambda e: e.activation(erow2, erow2, AF.Ln, bias=np.float32(1.0)),
                         reads=['rowsc'], writes=['rowsc'])
                    K.op('dve', lambda e: e.tensor_scalar(erow, erow, 0.0, None,
                                                          ALU.max, ALU.bypass),
                         reads=['rowsc'], writes=['rowsc'])
                    K.op('dve', lambda e: e.tensor_tensor(erow, erow, erow2, ALU.add),
                         reads=['rowsc'], writes=['rowsc'])
                    K.op('qs', lambda e, n=n: e.dma_start(
                        out=erow2, in_=cm_rows[n:n + 1, :]), writes=['rowsc'])
                    K.op('dve', lambda e: e.tensor_tensor(erow, erow, erow2, ALU.mult),
                         reads=['rowsc'], writes=['rowsc'])
                    K.op('act', lambda e: e.activation(erow_bf[:], erow, AF.Copy),
                         reads=['rowsc'], writes=['erow_bf'])
                    for vb in range(4):
                        K.op('pe', lambda e, n=n, vb=vb: e.matmul(
                            ps_a[:, 0:512], ones_row[:],
                            erow_bf[:, vb * 512:(vb + 1) * 512], start=True, stop=True),
                            reads=['erow_bf', 'ones_row'], writes=['psa'])
                        K.op('act', lambda e, n=n, vb=vb: e.activation(
                            ebc[:, n * DI + vb * 512: n * DI + (vb + 1) * 512],
                            ps_a[:, 0:512], AF.Copy),
                            reads=['psa'], writes=['ebc'])

            K.barrier()
            with contextlib.ExitStack() as SW:
                sw = lambda n, sh, dt=F32: SW.enter_context(nc.sbuf_tensor(n, sh, dt))
                xin_t = sw("xin_t", [128, DM])
                h1tm = sw("h1tm", [128, DM], BF16)
                h1fm = [sw(f"h1fm{i}", [128, 8 * 128], BF16) for i in range(3)]
                gdt = [sw(f"gdt{i}", [NS, 512]) for i in range(3)]
                ggm = [sw(f"ggm{i}", [1, 512]) for i in range(3)]
                bn12 = sw("bn12", [128, 12])
                bn2t = sw("bn2t", [128, 2])
                mcol = sw("mcol", [128, 1]); rcol = sw("rcol", [128, 1])
                xtm = [sw("xtm0", [128, DI], BF16)]; xtm = xtm * 2
                xet = [sw("xet0", [128, DI], BF16)]; xet = xet * 2
                dnt = [sw(f"dnt{i}", [128, 128], BF16) for i in range(2)]
                rhsn = [sw(f"rhsn{i}", [128, 128], BF16) for i in range(2)]
                rest = sw("rest", [128, DI], BF16)
                ylt = sw("ylt", [128, DI], BF16)
                wres_s = [sw("wres_s0", [128, 4096], BF16)]; wres_s = wres_s * 2
                dtg = sw("dtg", [NS, 128])
                ggr = sw("ggr", [1, 128])
                sp1 = sw("sp1", [NS, 128]); sp2 = sw("sp2", [NS, 128])
                lnm = sw("lnm", [NS, 128])
                ltm = sw("ltm", [128, NS])
                cnm = sw("cnm", [NS, 128])
                pnm = sw("pnm", [NS, 128], BF16)
                plt = sw("plt", [NS, 128], BF16)
                e127 = sw("e127", [NS, 1])
                e2nm = sw("e2nm", [NS, 128], BF16)
                e2tm = sw("e2tm", [128, NS], BF16)
                vbt = sw("vbt", [NS, 512])
                grow = sw("grow", [1, 128]); gsig = sw("gsig", [1, 128])
                gcol_ps = sw("gcol_ps", [128, 1], BF16)

                def stats(src_tile, src_region, width):
                    nb = (width + 511) // 512
                    for i in range(nb):
                        lo = i * 512; hi = min(width, lo + 512)
                        K.op('dve', lambda e, lo=lo, hi=hi, i=i: e.bn_stats(
                            bn12[:, 6 * i:6 * i + 6], src_tile[:, lo:hi]),
                            reads=[src_region], writes=['bn12'])
                    K.op('dve', lambda e, nb=nb: e.bn_aggr(bn2t[:], bn12[:, 0:6 * nb]),
                         reads=['bn12'], writes=['bn2t'])
                    K.op('dve', lambda e: e.tensor_copy(mcol[:], bn2t[:, 0:1]),
                         reads=['bn2t'], writes=['mcol'])
                    K.op('act', lambda e: e.activation(rcol[:], bn2t[:, 1:2], AF.Ln, bias=np.float32(EPS)),
                         reads=['bn2t'], writes=['rcol'])
                    K.op('act', lambda e: e.activation(rcol[:], rcol[:], AF.Exp, scale=np.float32(-0.5)),
                         reads=['rcol'], writes=['rcol'])

                def wres_load(ob4):
                    buf = wres_s[ob4 % 2]
                    K.op('qs', lambda e, ob4=ob4, buf=buf: e.dma_start(
                        out=buf[:].rearrange("p (k o) -> p k o", k=8),
                        in_=wres_h[:, :, ob4 * 512:(ob4 + 1) * 512]
                        .rearrange("k p o -> p k o")),
                        reads=['wres_h'], writes=['wres_s0'])
                wres_load(0)

                for j in range(NE):
                    if j == 0:
                        src = xl
                    elif j == NE - 1:
                        src = xr
                    else:
                        src = x_in[(j - 1) * L: j * L, :]
                    K.op('qs', lambda e, src=src: e.dma_start(out=xin_t[:], in_=src),
                         writes=['xin_t'])
                    stats(xin_t, 'xin_t', DM)
                    K.op('dve', lambda e: e.tensor_scalar(
                        h1tm[:], xin_t[:], mcol[:, 0:1], rcol[:, 0:1],
                        ALU.subtract, ALU.mult),
                        reads=['xin_t', 'mcol', 'rcol'], writes=['h1tm'])
                    slot = j % 3
                    hf = h1fm[slot]; hfr = f'h1fm{slot}'

                    def t8(e, hf=hf):
                        inst = None
                        for ic in range(8):
                            inst = e.transpose(ps_t[:, ic * 128:(ic + 1) * 128],
                                               h1tm[:, ic * 128:(ic + 1) * 128], idb[:])
                        return inst
                    K.op('pe', t8, reads=['h1tm', 'idb'], writes=['pst'])
                    K.op('act', lambda e, hf=hf: e.activation(hf[:], ps_t[:], AF.Copy),
                         reads=['pst'], writes=[hfr])
                    def gmm(e, hf=hf):
                        inst = None
                        for k in range(4):
                            for ic in range(8):
                                inst = e.matmul(
                                    ps_d[0:NS, k * 128:(k + 1) * 128],
                                    dtw_all[:, (k * 8 + ic) * NS:(k * 8 + ic + 1) * NS],
                                    hf[:, ic * 128:(ic + 1) * 128],
                                    start=(ic == 0), stop=(ic == 7))
                        return inst
                    K.op('pe', gmm, reads=[hfr, 'dtw_all'], writes=PSD_ALL)
                    K.op('act', lambda e, g=gdt[slot]: e.activation(
                        g[:], ps_d[0:NS, :], AF.Copy), reads=PSD_ALL,
                        writes=[f'gdt{slot}'])

                    def gmm2(e, hf=hf):
                        inst = None
                        for k in range(4):
                            for ic in range(8):
                                inst = e.matmul(
                                    ps_d[0:1, k * 128:(k + 1) * 128],
                                    xpw_all[:, k * 8 + ic:k * 8 + ic + 1],
                                    hf[:, ic * 128:(ic + 1) * 128],
                                    start=(ic == 0), stop=(ic == 7))
                        return inst
                    K.op('pe', gmm2, reads=[hfr, 'xpw_all'], writes=PSD_ALL)
                    K.op('act', lambda e, g=ggm[slot]: e.activation(
                        g[:], ps_d[0:1, :], AF.Copy), reads=PSD_ALL,
                        writes=[f'ggm{slot}'])

                    if j < 2:
                        continue
                    c = j - 2
                    e_i = c + 1
                    sm1, s0, sp1s = (e_i - 1) % 3, e_i % 3, (e_i + 1) % 3
                    gm1, g0, gp1 = gdt[sm1], gdt[s0], gdt[sp1s]
                    rm1, r0, rp1 = f'gdt{sm1}', f'gdt{s0}', f'gdt{sp1s}'
                    hm0 = h1fm[s0]; hm0r = f'h1fm{s0}'
                    p = c % 2

                    K.op('dve', lambda e, g0=g0: e.tensor_copy(dtg[:, 1:128], g0[:, 0:127]),
                         reads=[r0], writes=['dtg'])
                    K.op('dve', lambda e, gm1=gm1: e.tensor_copy(dtg[:, 0:1], gm1[:, 127:128]),
                         reads=[rm1], writes=['dtg'])
                    K.op('dve', lambda e, g0=g0: e.tensor_tensor(
                        dtg[:], dtg[:], g0[:, 128:256], ALU.add), reads=['dtg', r0], writes=['dtg'])
                    K.op('dve', lambda e, g0=g0: e.tensor_tensor(
                        dtg[:, 0:127], dtg[:, 0:127], g0[:, 257:384], ALU.add),
                        reads=['dtg', r0], writes=['dtg'])
                    K.op('dve', lambda e, gp1=gp1: e.tensor_tensor(
                        dtg[:, 127:128], dtg[:, 127:128], gp1[:, 256:257], ALU.add),
                        reads=['dtg', rp1], writes=['dtg'])
                    K.op('dve', lambda e, g0=g0: e.tensor_tensor(
                        dtg[:, 0:126], dtg[:, 0:126], g0[:, 386:512], ALU.add),
                        reads=['dtg', r0], writes=['dtg'])
                    K.op('dve', lambda e, gp1=gp1: e.tensor_tensor(
                        dtg[:, 126:128], dtg[:, 126:128], gp1[:, 384:386], ALU.add),
                        reads=['dtg', rp1], writes=['dtg'])
                    gg0, ggm1, ggp1 = ggm[s0], ggm[sm1], ggm[sp1s]
                    q0, qm1, qp1 = f'ggm{s0}', f'ggm{sm1}', f'ggm{sp1s}'
                    K.op('dve', lambda e, gg0=gg0: e.tensor_copy(ggr[:, 1:128], gg0[:, 0:127]),
                         reads=[q0], writes=['ggr'])
                    K.op('dve', lambda e, ggm1=ggm1: e.tensor_copy(ggr[:, 0:1], ggm1[:, 127:128]),
                         reads=[qm1], writes=['ggr'])
                    K.op('dve', lambda e, gg0=gg0: e.tensor_tensor(
                        ggr[:], ggr[:], gg0[:, 128:256], ALU.add), reads=['ggr', q0], writes=['ggr'])
                    K.op('dve', lambda e, gg0=gg0: e.tensor_tensor(
                        ggr[:, 0:127], ggr[:, 0:127], gg0[:, 257:384], ALU.add),
                        reads=['ggr', q0], writes=['ggr'])
                    K.op('dve', lambda e, ggp1=ggp1: e.tensor_tensor(
                        ggr[:, 127:128], ggr[:, 127:128], ggp1[:, 256:257], ALU.add),
                        reads=['ggr', qp1], writes=['ggr'])
                    K.op('dve', lambda e, gg0=gg0: e.tensor_tensor(
                        ggr[:, 0:126], ggr[:, 0:126], gg0[:, 386:512], ALU.add),
                        reads=['ggr', q0], writes=['ggr'])
                    K.op('dve', lambda e, ggp1=ggp1: e.tensor_tensor(
                        ggr[:, 126:128], ggr[:, 126:128], ggp1[:, 384:386], ALU.add),
                        reads=['ggr', qp1], writes=['ggr'])

                    K.op('act', lambda e: e.activation(sp1[:], dtg[:], AF.Abs,
                                                       bias=dtcb[:, 0:1]),
                         reads=['dtg', 'dtcb'], writes=['sp1r'])
                    K.op('act', lambda e: e.activation(sp1[:], sp1[:], AF.Exp, scale=np.float32(-1.0)),
                         reads=['sp1r'], writes=['sp1r'])
                    K.op('act', lambda e: e.activation(sp1[:], sp1[:], AF.Ln, bias=np.float32(1.0)),
                         reads=['sp1r'], writes=['sp1r'])
                    K.op('dve', lambda e: e.tensor_scalar(sp2[:], dtg[:], dtcb[:, 0:1], 0.0,
                                                          ALU.add, ALU.max),
                         reads=['dtg', 'dtcb'], writes=['sp2r'])
                    K.op('dve', lambda e: e.tensor_tensor(sp2[:], sp1[:], sp2[:], ALU.add),
                         reads=['sp1r', 'sp2r'], writes=['sp2r'])
                    K.op('dve', lambda e: e.tensor_scalar(lnm[:], sp2[:], spa_col[:, 0:1],
                                                          -1.0, ALU.mult, ALU.mult),
                         reads=['sp2r', 'spa_col'], writes=['lnm'])

                    K.op('dve', lambda e: e.tensor_scalar(ggr[:], ggr[:], gamc[0:1, 0:1],
                                                          None, ALU.add, ALU.bypass),
                         reads=['ggr', 'gamc'], writes=['ggr'])
                    K.op('act', lambda e: e.activation(gsig[:], ggr[:], AF.Exp, scale=np.float32(-1.0)),
                         reads=['ggr'], writes=['gsig'])
                    K.op('dve', lambda e: e.tensor_scalar(gsig[:], gsig[:], 1.0, None,
                                                          ALU.add, ALU.bypass),
                         reads=['gsig'], writes=['gsig'])
                    K.op('dve', lambda e: e.reciprocal(gsig[:], gsig[:]),
                         reads=['gsig'], writes=['gsig'])
                    K.op('dve', lambda e: e.tensor_tensor(grow[:], ggr[:], gsig[:], ALU.mult),
                         reads=['ggr', 'gsig'], writes=['grow'])
                    K.op('pe', lambda e: e.transpose(ps_d[:, 0:1], grow[:], idf[0:1, 0:1]),
                         reads=['grow', 'idf'], writes=PSD_ALL)
                    K.op('act', lambda e, c=c: e.activation(gcols[:, c:c + 1],
                                                            ps_d[:, 0:1], AF.Copy),
                         reads=PSD_ALL, writes=['gcols'])

                    K.op('dve', lambda e: e.tensor_tensor_scan(
                        cnm[:], ones16[:], lnm[:], 0.0, ALU.mult, ALU.add),
                        reads=['ones16', 'lnm'], writes=['cnm'])
                    K.op('act', lambda e: e.activation(pnm[:], cnm[:], AF.Exp),
                         reads=['cnm'], writes=['pnm'])
                    K.op('act', lambda e: e.activation(e127[:], cnm[:, 127:128], AF.Exp),
                         reads=['cnm'], writes=['e127'])
                    K.op('dve', lambda e: e.tensor_scalar(plt[:], pnm[:], lam[:, 0:1],
                                                          None, ALU.mult, ALU.bypass),
                         reads=['pnm', 'lam'], writes=['plt'])
                    K.op('qg', lambda e, c=c: e.dma_start(out=pl_h[c], in_=plt[:]),
                         reads=['plt'], writes=['pl_h'])
                    K.op('dve', lambda e: e.tensor_scalar(lam[:], lam[:], e127[:, 0:1],
                                                          None, ALU.mult, ALU.bypass),
                         reads=['lam', 'e127'], writes=['lam'])
                    K.op('dve', lambda e: e.tensor_scalar(
                        sp1[:], cnm[:], cnm[:, 127:128], -1.0, ALU.subtract, ALU.mult),
                        reads=['cnm'], writes=['sp1r'])
                    K.op('act', lambda e: e.activation(e2nm[:], sp1[:], AF.Exp),
                         reads=['sp1r'], writes=['e2nm'])
                    K.op('pe', lambda e: e.transpose(ps_t[:, 0:NS], e2nm[:], idb[0:NS, 0:NS]),
                         reads=['e2nm', 'idb'], writes=['pst'])
                    K.op('act', lambda e: e.activation(e2tm[:], ps_t[:, 0:NS], AF.Copy),
                         reads=['pst'], writes=['e2tm'])
                    K.op('pe', lambda e: e.transpose(ps_d[:, 0:NS], lnm[:], idf[0:NS, 0:NS]),
                         reads=['lnm', 'idf'], writes=PSD_ALL)
                    K.op('dve', lambda e: e.tensor_copy(ltm[:], ps_d[:, 0:NS]),
                         reads=PSD_ALL, writes=['ltm'])

                    xt = xtm[0]; xtr = 'xtm0'
                    for ob in range(8):
                        pst_t = ps_a if ob % 2 == 0 else ps_b
                        pst_r = 'psa' if ob % 2 == 0 else 'psb'
                        ws = wres_s[ob % 2] if ob >= 4 else None
                        wsr = 'wres_s0'

                        def ipmm(e, ob=ob, pst_t=pst_t, hm0=hm0, ws=ws):
                            inst = None
                            for kc in range(8):
                                if ob < 4:
                                    rhs = wxc_bf[:, kc * DI + ob * 512: kc * DI + (ob + 1) * 512]
                                else:
                                    rhs = ws[:, kc * 512:(kc + 1) * 512]
                                inst = e.matmul(pst_t[:], hm0[:, kc * 128:(kc + 1) * 128],
                                                rhs, start=(kc == 0), stop=False)
                            bcol = inb_bf[0:1, ob * 512:(ob + 1) * 512]
                            inst = e.matmul(pst_t[:], ones_row[:], bcol,
                                            start=False, stop=True)
                            return inst
                        K.op('pe', ipmm,
                             reads=[hm0r, 'wxc_bf' if ob < 4 else wsr, 'inb_bf', 'ones_row'],
                             writes=[pst_r])
                        if ob >= 4:
                            nxt = (ob - 3) % 4
                            wres_load(nxt)
                        if ob < 4:
                            K.op('act', lambda e, ob=ob, pst_t=pst_t, xt=xt: e.activation(
                                xt[:, ob * 512:(ob + 1) * 512], pst_t[:], AF.Copy),
                                reads=[pst_r], writes=[xtr])
                        else:
                            K.op('act', lambda e, ob=ob, pst_t=pst_t: e.activation(
                                rest[:, (ob - 4) * 512:(ob - 3) * 512], pst_t[:], AF.Copy),
                                reads=[pst_r], writes=['rest'])
                    K.op('qg', lambda e, c=c: e.dma_start(out=res_h[c], in_=rest[:]),
                         reads=['rest'], writes=['res_h'])

                    for n in range(NS):
                        pn2 = n % 2
                        xe = xet[0]; xer = 'xet0'
                        dn = dnt[pn2]; dnr = f'dnt{pn2}'
                        rn = rhsn[pn2]; rnr = f'rhsn{pn2}'
                        slot_d = n % 4
                        psdr = f'psdd{slot_d}'
                        K.op('dve', lambda e, n=n, xe=xe, xt=xt: e.tensor_tensor(
                            xe[:], xt[:], ebc[:, n * DI:(n + 1) * DI], ALU.mult),
                            reads=[xtr, 'ebc'], writes=[xer])
                        K.op('dve', lambda e, n=n, rn=rn: e.tensor_scalar(
                            rn[:], rlet[:], ltm[:, n:n + 1], None, ALU.mult, ALU.bypass),
                            reads=['rlet', 'ltm'], writes=[rnr])

                        def dmm(e, slot_d=slot_d, rn=rn):
                            e.matmul(ps_d[:, slot_d * 128:(slot_d + 1) * 128],
                                     rgts[:], rn[:], start=True, stop=False)
                            return e.matmul(ps_d[:, slot_d * 128:(slot_d + 1) * 128],
                                            idb[:], mnegt[:], start=False, stop=True)
                        K.op('pe', dmm, reads=['rgts', rnr, 'idb', 'mnegt'], writes=[psdr])
                        K.op('act', lambda e, slot_d=slot_d, dn=dn: e.activation(
                            dn[:], ps_d[:, slot_d * 128:(slot_d + 1) * 128], AF.Exp),
                            reads=[psdr], writes=[dnr])

                        def ymm(e, n=n, dn=dn, xe=xe):
                            inst = None
                            for vb in range(4):
                                inst = e.matmul(ps_y[:, vb * 512:(vb + 1) * 512],
                                                dn[:], xe[:, vb * 512:(vb + 1) * 512],
                                                start=(n == 0), stop=False)
                            return inst
                        K.op('pe', ymm, reads=[dnr, xer], writes=['psy'])

                    K.op('dve', lambda e: e.tensor_tensor(hc[:], hstate[:], cmt_s[:], ALU.mult),
                         reads=['H', 'cmt'], writes=['hcr'])

                    def y2mm(e):
                        inst = None
                        for vb in range(4):
                            inst = e.matmul(ps_y[:, vb * 512:(vb + 1) * 512],
                                            pnm[:], hc[0:NS, vb * 512:(vb + 1) * 512],
                                            start=False, stop=True)
                        return inst
                    K.op('pe', y2mm, reads=['pnm', 'hcr'], writes=['psy'])
                    K.op('act', lambda e: e.activation(ylt[:], ps_y[:], AF.Copy),
                         reads=['psy'], writes=['ylt'])
                    K.op('qg', lambda e, c=c: e.dma_start(out=yloc_h[c], in_=ylt[:]),
                         reads=['ylt'], writes=['yloc_h'])

                    for vb in range(4):
                        K.op('pe', lambda e, vb=vb, xt=xt: e.matmul(
                            ps_b[0:NS, 0:512], e2tm[:],
                            xt[:, vb * 512:(vb + 1) * 512], start=True, stop=True),
                            reads=['e2tm', xtr], writes=['psb'])
                        K.op('dve', lambda e, vb=vb: e.tensor_tensor(
                            vbt[:], ps_b[0:NS, 0:512],
                            dbt[0:NS, vb * 512:(vb + 1) * 512], ALU.mult),
                            reads=['psb', 'dbt'], writes=['vbtr'])
                        K.op('dve', lambda e, vb=vb: e.scalar_tensor_tensor(
                            hstate[0:NS, vb * 512:(vb + 1) * 512],
                            hstate[0:NS, vb * 512:(vb + 1) * 512],
                            e127[:, 0:1], vbt[:], ALU.mult, ALU.add),
                            reads=['H', 'e127', 'vbtr'], writes=['H'])

            K.barrier()
            with contextlib.ExitStack() as SX:
                sx = lambda n, sh, dt=F32: SX.enter_context(nc.sbuf_tensor(n, sh, dt))
                K.op('qg', lambda e: e.dma_start(out=ag_in[:], in_=hstate[:]),
                     reads=['H'], writes=['ag_in_r'])
                K.op('cc', lambda e: e.collective_compute(
                    "AllGather", ALU.bypass, replica_groups=[list(range(8))],
                    ins=[ag_in_t.ap().opt()], outs=[ag_out_t.ap().opt()]),
                    reads=['ag_in_r'], writes=['ag_out_r'])
                ags = sx("ags", [128, DI])
                ags_bf = sx("ags_bf", [128, DI], BF16)
                selm = sx("selm", [128, NS], BF16)
                selm_f = sx("selm_f", [128, NS])
                h0t = sx("h0t", [NS, DI])
                K.op('qg', lambda e: e.dma_start(out=ags[:], in_=ag_out[:]),
                     reads=['ag_out_r'], writes=['ags'])
                K.op('act', lambda e: e.activation(ags_bf[:], ags[:], AF.Copy),
                     reads=['ags'], writes=['ags_bf'])
                K.op('qs', lambda e: e.dma_start(out=selm_f[:], in_=selmat[:]),
                     writes=['selm_f'])
                K.op('act', lambda e: e.activation(selm[:], selm_f[:], AF.Copy),
                     reads=['selm_f'], writes=['selm'])
                for vb in range(4):
                    K.op('pe', lambda e, vb=vb: e.matmul(
                        ps_a[0:NS, 0:512], selm[:],
                        ags_bf[:, vb * 512:(vb + 1) * 512], start=True, stop=True),
                        reads=['selm', 'ags_bf'], writes=['psa'])
                    K.op('dve', lambda e, vb=vb: e.tensor_copy(
                        h0t[0:NS, vb * 512:(vb + 1) * 512], ps_a[0:NS, 0:512]),
                        reads=['psa'], writes=['h0t'])
                K.op('dve', lambda e: e.tensor_tensor(h0c[:], h0t[:], cmt_s[:], ALU.mult),
                     reads=['h0t', 'cmt'], writes=['h0c'])
                K.barrier()

        with contextlib.ExitStack() as SB:
            sb1 = lambda n, sh, dt=F32: SB.enter_context(nc.sbuf_tensor(n, sh, dt))
            owt_bf = sb1("owt_bf", [128, 16 * DM], BF16)
            outb_bf = sb1("outb_bf", [1, DM], BF16)
            ylti = [sb1(f"ylti{i}", [128, DI], BF16) for i in range(2)]
            resi = [sb1(f"resi{i}", [128, DI], BF16) for i in range(2)]
            xi = [sb1(f"xi{i}", [128, DM]) for i in range(2)]
            plti = [sb1(f"plti{i}", [NS, 128], BF16) for i in range(2)]
            ygt = sb1("ygt", [128, DI], BF16)
            prefm = sb1("prefm", [128, DI], BF16)
            mo = sb1("mo", [128, DM])
            x2t = sb1("x2t", [128, DM])
            bn12b = sb1("bn12b", [128, 12])
            bn2b = sb1("bn2b", [128, 2])
            m3 = sb1("m3", [128, 1]); r3 = sb1("r3", [128, 1])
            stgo = [sb1("stgo0", [128, DM]), sb1("stgo1", [128, DM])]

            for kc in range(16):
                s = stgo[kc % 2]; rs = f'stgo{kc % 2}'
                K.op('qs', lambda e, kc=kc, s=s: e.dma_start(
                    out=s[:], in_=out_wT[kc * 128:(kc + 1) * 128, :]), writes=[rs])
                K.op('act', lambda e, kc=kc, s=s: e.activation(
                    owt_bf[:, kc * DM:(kc + 1) * DM], s[:], AF.Copy),
                    reads=[rs], writes=['owt_bf'])
            K.op('qs', lambda e: e.dma_start(out=stgo[0][0:1, :], in_=out_b[:]),
                 writes=['stgo0'])
            K.op('act', lambda e: e.activation(outb_bf[:], stgo[0][0:1, :], AF.Copy),
                 reads=['stgo0'], writes=['outb_bf'])

            for c in range(nch):
                p = c % 2
                yt, ytr = ylti[p], f'ylti{p}'
                rt, rtr = resi[p], f'resi{p}'
                xt2, xtr2 = xi[p], f'xi{p}'
                pt, ptr = plti[p], f'plti{p}'
                K.op('qs', lambda e, c=c, yt=yt: e.dma_start(out=yt[:], in_=yloc_h[c]),
                     reads=['yloc_h'], writes=[ytr])
                K.op('qs', lambda e, c=c, rt=rt: e.dma_start(out=rt[:], in_=res_h[c]),
                     reads=['res_h'], writes=[rtr])
                K.op('qs', lambda e, c=c, xt2=xt2: e.dma_start(
                    out=xt2[:], in_=x_in[c * L:(c + 1) * L, :]), writes=[xtr2])
                K.op('qs', lambda e, c=c, pt=pt: e.dma_start(out=pt[:], in_=pl_h[c]),
                     reads=['pl_h'], writes=[ptr])

                def cmm(e, pt=pt, yt=yt):
                    inst = None
                    for vb in range(4):
                        e.matmul(ps_y[:, vb * 512:(vb + 1) * 512], pt[:],
                                 h0c[0:NS, vb * 512:(vb + 1) * 512],
                                 start=True, stop=False)
                        inst = e.matmul(ps_y[:, vb * 512:(vb + 1) * 512], idb[:],
                                        yt[:, vb * 512:(vb + 1) * 512],
                                        start=False, stop=True)
                    return inst
                K.op('pe', cmm, reads=[ptr, 'h0c', 'idb', ytr], writes=['psy'])
                K.op('dve', lambda e, c=c: e.tensor_scalar(
                    ygt[:], ps_y[:], gcols[:, c:c + 1], None, ALU.mult, ALU.bypass),
                    reads=['psy', 'gcols'], writes=['ygt'])
                K.op('dve', lambda e, rt=rt: e.tensor_tensor(ygt[:], ygt[:], rt[:], ALU.add),
                     reads=['ygt', rtr], writes=['ygt'])
                for rnd in range(2):
                    def tr8(e, rnd=rnd):
                        inst = None
                        for i in range(8):
                            blk_i = rnd * 8 + i
                            inst = e.transpose(ps_t[:, i * 128:(i + 1) * 128],
                                               ygt[:, blk_i * 128:(blk_i + 1) * 128], idb[:])
                        return inst
                    K.op('pe', tr8, reads=['ygt', 'idb'], writes=['pst'])
                    K.op('act', lambda e, rnd=rnd: e.activation(
                        prefm[:, rnd * 1024:(rnd + 1) * 1024], ps_t[:], AF.Copy),
                        reads=['pst'], writes=['prefm'])
                for ob in range(2):
                    pst_t = ps_a if ob == 0 else ps_b
                    pst_r = 'psa' if ob == 0 else 'psb'

                    def opmm(e, ob=ob, pst_t=pst_t):
                        inst = None
                        for kc in range(16):
                            inst = e.matmul(pst_t[:],
                                            prefm[:, kc * 128:(kc + 1) * 128],
                                            owt_bf[:, kc * DM + ob * 512: kc * DM + (ob + 1) * 512],
                                            start=(kc == 0), stop=False)
                        inst = e.matmul(pst_t[:], ones_row[:],
                                        outb_bf[0:1, ob * 512:(ob + 1) * 512],
                                        start=False, stop=True)
                        return inst
                    K.op('pe', opmm, reads=['prefm', 'owt_bf', 'outb_bf', 'ones_row'],
                         writes=[pst_r])
                    K.op('act', lambda e, ob=ob, pst_t=pst_t: e.activation(
                        mo[:, ob * 512:(ob + 1) * 512], pst_t[:], AF.Copy),
                        reads=[pst_r], writes=['mo'])
                for i in range(2):
                    K.op('dve', lambda e, i=i: e.bn_stats(
                        bn12b[:, 6 * i:6 * i + 6], mo[:, i * 512:(i + 1) * 512]),
                        reads=['mo'], writes=['bn12b'])
                K.op('dve', lambda e: e.bn_aggr(bn2b[:], bn12b[:, 0:12]),
                     reads=['bn12b'], writes=['bn2b'])
                K.op('dve', lambda e: e.tensor_copy(m3[:], bn2b[:, 0:1]),
                     reads=['bn2b'], writes=['m3'])
                K.op('act', lambda e: e.activation(r3[:], bn2b[:, 1:2], AF.Ln, bias=np.float32(EPS)),
                     reads=['bn2b'], writes=['r3'])
                K.op('act', lambda e: e.activation(r3[:], r3[:], AF.Exp, scale=np.float32(-0.5)),
                     reads=['r3'], writes=['r3'])
                K.op('dve', lambda e: e.tensor_scalar(x2t[:], mo[:], m3[:, 0:1],
                                                      r3[:, 0:1], ALU.subtract, ALU.mult),
                     reads=['mo', 'm3', 'r3'], writes=['x2t'])
                K.op('dve', lambda e, xt2=xt2: e.tensor_tensor(x2t[:], x2t[:], xt2[:], ALU.add),
                     reads=['x2t', xtr2], writes=['x2t'])
                K.op('qg', lambda e, c=c: e.dma_start(out=x2_h[c], in_=x2t[:]),
                     reads=['x2t'], writes=['x2_h'])
            K.barrier()

        with contextlib.ExitStack() as SQ:
            sq = lambda n, sh, dt=F32: SQ.enter_context(nc.sbuf_tensor(n, sh, dt))
            qst = [sq("qst0", [128, DF]), sq("qst1", [128, DF])]
            qtt = sq("qtt", [128, DF])
            qbf = sq("qbf", [128, DF], BF16)
            accc = sq("accc", [128, 1])
            onescol = sq("onescol", [128, 1])
            tot = sq("tot", [1, 1])
            K.op('dve', lambda e: e.memset(onescol[:], 1.0), writes=['onescol'])
            for wn, wsrc, wdst, nkc, odim in (
                    ('g', wgT, wgq_h, 8, DF), ('u', wuT, wuq_h, 8, DF),
                    ('d', wdT, wdq_h, 32, DM)):
                K.op('dve', lambda e: e.memset(accc[:], 0.0), writes=['accc'])
                for kc in range(nkc):
                    s = qst[kc % 2]; rs = f'qst{kc % 2}'
                    K.op('qs', lambda e, kc=kc, s=s, wsrc=wsrc, odim=odim: e.dma_start(
                        out=s[:, 0:odim], in_=wsrc[kc * 128:(kc + 1) * 128, :]), writes=[rs])
                    K.op('dve', lambda e, s=s, odim=odim: e.tensor_reduce(
                        qtt[:, 0:1], s[:, 0:odim], AX.X, ALU.add,
                        apply_absolute_value=True),
                        reads=[rs], writes=['qtt1'])
                    K.op('dve', lambda e: e.tensor_tensor(accc[:], accc[:], qtt[:, 0:1],
                                                          ALU.add),
                         reads=['accc', 'qtt1'], writes=['accc'])
                K.op('pe', lambda e: e.matmul(ps_d[0:1, 0:1], accc[:], onescol[:],
                                              start=True, stop=True),
                     reads=['accc', 'onescol'], writes=PSD_ALL)
                nel = float(nkc * 128 * odim)
                K.op('dve', lambda e, wn=wn, nel=nel: e.tensor_scalar(
                    sQ[wn + 'c'][:], ps_d[0:1, 0:1], 1.0 / nel, 1e-5, ALU.mult, ALU.max),
                    reads=PSD_ALL, writes=[f'cw{wn}'])
                K.op('dve', lambda e, wn=wn: e.reciprocal(sQ[wn][:], sQ[wn + 'c'][:]),
                     reads=[f'cw{wn}'], writes=[f'sw{wn}'])
                K.op('pe', lambda e, wn=wn: e.matmul(
                    ps_d[:, 0:1], ones_row_f[:], sQ[wn][:], start=True, stop=True),
                    reads=[f'sw{wn}', 'ones_row_f'], writes=PSD_ALL)
                K.op('dve', lambda e, wn=wn: e.tensor_copy(
                    sQ[wn + 'col'][:], ps_d[:, 0:1]),
                    reads=PSD_ALL, writes=[f'swc{wn}'])
                for kc in range(nkc):
                    s = qst[kc % 2]; rs = f'qst{kc % 2}'
                    K.op('qs', lambda e, kc=kc, s=s, wsrc=wsrc, odim=odim: e.dma_start(
                        out=s[:, 0:odim], in_=wsrc[kc * 128:(kc + 1) * 128, :]), writes=[rs])
                    K.op('dve', lambda e, s=s, wn=wn, odim=odim: e.tensor_scalar(
                        qtt[:, 0:odim], s[:, 0:odim], sQ[wn + 'col'][:, 0:1], MAGIC,
                        ALU.mult, ALU.add),
                        reads=[rs, f'swc{wn}'], writes=['qtt1'])
                    K.op('dve', lambda e, odim=odim: e.tensor_scalar(
                        qtt[:, 0:odim], qtt[:, 0:odim], MAGIC, -1.0,
                        ALU.subtract, ALU.max),
                        reads=['qtt1'], writes=['qtt1'])
                    K.op('dve', lambda e, odim=odim: e.tensor_scalar(
                        qbf[:, 0:odim], qtt[:, 0:odim], 1.0, None, ALU.min, ALU.bypass),
                        reads=['qtt1'], writes=['qbf'])
                    K.op('qg', lambda e, kc=kc, wdst=wdst, odim=odim: e.dma_start(
                        out=wdst[kc], in_=qbf[:, 0:odim]),
                        reads=['qbf'], writes=[f'wq{wn}_h'])
            K.barrier()

        with contextlib.ExitStack() as SF:
            sf = lambda n, sh, dt=F32: SF.enter_context(nc.sbuf_tensor(n, sh, dt))
            wgq_s = sf("wgq_s", [128, 8 * DF], BF16)
            wuq_s = sf("wuq_s", [128, 8 * DF], BF16)
            x2i = [sf(f"x2i{i}", [128, DM]) for i in range(2)]
            q1tm = sf("q1tm", [128, DM], BF16)
            q1fm = sf("q1fm", [128, DM], BF16)
            gatet = sf("gatet", [128, DF], BF16)
            upt = sf("upt", [128, DF], BF16)
            gut = sf("gut", [128, DF], BF16)
            t5h = sf("t5h", [128, DI])
            q2tm = sf("q2tm", [128, DF], BF16)
            q2fm = sf("q2fm", [128, DF], BF16)
            bn12c = sf("bn12c", [128, 12])
            bn2c = sf("bn2c", [128, 2])
            m4 = sf("m4", [128, 1]); r4 = sf("r4", [128, 1])
            mxc = sf("mxc", [128, 1]); mnc = sf("mnc", [128, 1])
            s4 = sf("s4", [128, 1]); u4 = sf("u4", [128, 1])
            gsc = sf("gsc", [128, 1]); usc = sf("usc", [128, 1])
            s2c = sf("s2c", [128, 1]); is4 = sf("is4", [128, 1])
            mg = sf("mg", [128, 1])

            K.op('qs', lambda e: e.dma_start(
                out=wgq_s[:].rearrange("p (k o) -> p k o", k=8),
                in_=wgq_h.rearrange("k p o -> p k o")),
                reads=['wqg_h'], writes=['wgq_s'])
            K.op('qs', lambda e: e.dma_start(
                out=wuq_s[:].rearrange("p (k o) -> p k o", k=8),
                in_=wuq_h.rearrange("k p o -> p k o")),
                reads=['wqu_h'], writes=['wuq_s'])
            for wn, col in (('g', cgcol), ('u', cucol), ('d', cdcol)):
                K.op('pe', lambda e, wn=wn: e.matmul(
                    ps_d[:, 0:1], ones_row_f[:], sQ[wn + 'c'][:], start=True, stop=True),
                    reads=[f'cw{wn}', 'ones_row_f'], writes=PSD_ALL)
                K.op('dve', lambda e, wn=wn, col=col: e.tensor_copy(
                    col[:], ps_d[:, 0:1]),
                    reads=PSD_ALL, writes=[f'{wn}ccol'])

            for c in range(nch):
                p = c % 2
                xt2, xtr2 = x2i[p], f'x2i{p}'
                K.op('qs', lambda e, c=c, xt2=xt2: e.dma_start(out=xt2[:], in_=x2_h[c]),
                     reads=['x2_h'], writes=[xtr2])
                for i in range(2):
                    K.op('dve', lambda e, i=i, xt2=xt2: e.bn_stats(
                        bn12c[:, 6 * i:6 * i + 6], xt2[:, i * 512:(i + 1) * 512]),
                        reads=[xtr2], writes=['bn12c'])
                K.op('dve', lambda e: e.bn_aggr(bn2c[:], bn12c[:, 0:12]),
                     reads=['bn12c'], writes=['bn2c'])
                K.op('dve', lambda e: e.tensor_copy(m4[:], bn2c[:, 0:1]),
                     reads=['bn2c'], writes=['m4'])
                K.op('act', lambda e: e.activation(r4[:], bn2c[:, 1:2], AF.Ln, bias=np.float32(EPS)),
                     reads=['bn2c'], writes=['r4'])
                K.op('act', lambda e: e.activation(r4[:], r4[:], AF.Exp, scale=np.float32(-0.5)),
                     reads=['r4'], writes=['r4'])
                K.op('dve', lambda e, xt2=xt2: e.tensor_reduce(
                    mxc[:], xt2[:], AX.X, ALU.max), reads=[xtr2], writes=['mxc'])
                K.op('dve', lambda e, xt2=xt2: e.tensor_reduce(
                    mnc[:], xt2[:], AX.X, ALU.min), reads=[xtr2], writes=['mnc'])
                K.op('dve', lambda e: e.tensor_scalar(mxc[:], mxc[:], m4[:, 0:1], None,
                                                      ALU.subtract, ALU.bypass),
                     reads=['mxc', 'm4'], writes=['mxc'])
                K.op('dve', lambda e: e.tensor_scalar(mnc[:], mnc[:], m4[:, 0:1], -1.0,
                                                      ALU.subtract, ALU.mult),
                     reads=['mnc', 'm4'], writes=['mnc'])
                K.op('dve', lambda e: e.tensor_tensor(mxc[:], mxc[:], mnc[:], ALU.max),
                     reads=['mxc', 'mnc'], writes=['mxc'])
                K.op('dve', lambda e: e.tensor_scalar(s4[:], mxc[:], r4[:, 0:1], 1e-5,
                                                      ALU.mult, ALU.max),
                     reads=['mxc', 'r4'], writes=['s4'])
                K.op('dve', lambda e: e.reciprocal(s4[:], s4[:]),
                     reads=['s4'], writes=['s4'])
                K.op('dve', lambda e: e.tensor_scalar(s4[:], s4[:], 127.0, None,
                                                      ALU.mult, ALU.bypass),
                     reads=['s4'], writes=['s4'])
                K.op('dve', lambda e: e.tensor_tensor(u4[:], r4[:], s4[:], ALU.mult),
                     reads=['r4', 's4'], writes=['u4'])
                K.op('dve', lambda e, xt2=xt2: e.tensor_scalar(
                    t5h[:, 0:DM], xt2[:], m4[:, 0:1], u4[:, 0:1], ALU.subtract, ALU.mult),
                    reads=[xtr2, 'm4', 'u4'], writes=['t5h'])
                K.op('dve', lambda e: e.tensor_scalar(q1tm[:], t5h[:, 0:DM], MAGIC, MAGIC,
                                                      ALU.add, ALU.subtract),
                     reads=['t5h'], writes=['q1tm'])

                def q1t(e):
                    inst = None
                    for ic in range(8):
                        inst = e.transpose(ps_t[:, ic * 128:(ic + 1) * 128],
                                           q1tm[:, ic * 128:(ic + 1) * 128], idb[:])
                    return inst
                K.op('pe', q1t, reads=['q1tm', 'idb'], writes=['pst'])
                K.op('act', lambda e: e.activation(q1fm[:], ps_t[:, 0:DM], AF.Copy),
                     reads=['pst'], writes=['q1fm'])
                K.op('dve', lambda e: e.reciprocal(is4[:], s4[:]),
                     reads=['s4'], writes=['is4'])
                K.op('dve', lambda e: e.tensor_tensor(gsc[:], is4[:], cgcol[:], ALU.mult),
                     reads=['is4', 'gccol'], writes=['gsc'])
                K.op('dve', lambda e: e.tensor_tensor(usc[:], is4[:], cucol[:], ALU.mult),
                     reads=['is4', 'uccol'], writes=['usc'])
                for fb in range(8):
                    def gamm(e, fb=fb):
                        inst = None
                        for kc in range(8):
                            inst = e.matmul(ps_a[:],
                                            q1fm[:, kc * 128:(kc + 1) * 128],
                                            wgq_s[:, kc * DF + fb * 512: kc * DF + (fb + 1) * 512],
                                            start=(kc == 0), stop=(kc == 7))
                        return inst
                    K.op('pe', gamm, reads=['q1fm', 'wgq_s'], writes=['psa'])
                    K.op('act', lambda e, fb=fb: e.activation(
                        gatet[:, fb * 512:(fb + 1) * 512], ps_a[:], AF.Silu,
                        scale=gsc[:, 0:1]),
                        reads=['psa', 'gsc'], writes=['gatet'])

                    def upmm(e, fb=fb):
                        inst = None
                        for kc in range(8):
                            inst = e.matmul(ps_b[:],
                                            q1fm[:, kc * 128:(kc + 1) * 128],
                                            wuq_s[:, kc * DF + fb * 512: kc * DF + (fb + 1) * 512],
                                            start=(kc == 0), stop=(kc == 7))
                        return inst
                    K.op('pe', upmm, reads=['q1fm', 'wuq_s'], writes=['psb'])
                    K.op('dve', lambda e, fb=fb: e.tensor_scalar(
                        upt[:, fb * 512:(fb + 1) * 512], ps_b[:], usc[:, 0:1], None,
                        ALU.mult, ALU.bypass),
                        reads=['psb', 'usc'], writes=['upt'])
                K.op('dve', lambda e: e.tensor_tensor(gut[:], gatet[:], upt[:], ALU.mult),
                     reads=['gatet', 'upt'], writes=['gut'])
                K.op('dve', lambda e: e.tensor_reduce(mg[:], gut[:], AX.X, ALU.max,
                                                      apply_absolute_value=True),
                     reads=['gut'], writes=['mg'])
                K.op('dve', lambda e: e.tensor_scalar(s2c[:], mg[:], 1e-5, None,
                                                      ALU.max, ALU.bypass),
                     reads=['mg'], writes=['s2c'])
                K.op('dve', lambda e: e.reciprocal(s2c[:], s2c[:]),
                     reads=['s2c'], writes=['s2c'])
                K.op('dve', lambda e: e.tensor_scalar(s2c[:], s2c[:], 127.0, None,
                                                      ALU.mult, ALU.bypass),
                     reads=['s2c'], writes=['s2c'])
                K.op('dve', lambda e, c=c: e.reciprocal(mg[:], s2c[:]),
                     reads=['s2c'], writes=['mg'])
                K.op('dve', lambda e, c=c: e.tensor_tensor(
                    ds2_all[:, c:c + 1], mg[:], cdcol[:], ALU.mult),
                    reads=['mg', 'dccol'], writes=['ds2_all'])
                for hh in range(2):
                    K.op('dve', lambda e, hh=hh: e.tensor_scalar(
                        t5h[:], gut[:, hh * DI:(hh + 1) * DI], s2c[:, 0:1], MAGIC,
                        ALU.mult, ALU.add),
                        reads=['gut', 's2c'], writes=['t5h'])
                    K.op('dve', lambda e, hh=hh: e.tensor_scalar(
                        q2tm[:, hh * DI:(hh + 1) * DI], t5h[:], MAGIC, None,
                        ALU.subtract, ALU.bypass),
                        reads=['t5h'], writes=['q2tm'])
                for rnd in range(4):
                    def q2t(e, rnd=rnd):
                        inst = None
                        for i in range(8):
                            blk_i = rnd * 8 + i
                            inst = e.transpose(ps_t[:, i * 128:(i + 1) * 128],
                                               q2tm[:, blk_i * 128:(blk_i + 1) * 128], idb[:])
                        return inst
                    K.op('pe', q2t, reads=['q2tm', 'idb'], writes=['pst'])
                    K.op('act', lambda e, rnd=rnd: e.activation(
                        q2fm[:, rnd * 1024:(rnd + 1) * 1024], ps_t[:], AF.Copy),
                        reads=['pst'], writes=['q2fm'])
                K.op('qg', lambda e, c=c: e.dma_start(out=q2_h[c], in_=q2fm[:]),
                     reads=['q2fm'], writes=['q2_h'])
            K.barrier()

        with contextlib.ExitStack() as SD:
            sd = lambda n, sh, dt=F32: SD.enter_context(nc.sbuf_tensor(n, sh, dt))
            wdq_s = sd("wdq_s", [128, 32 * DM], BF16)
            q2i = [sd(f"q2i{i}", [128, DF], BF16) for i in range(2)]
            x2j = [sd(f"x2j{i}", [128, DM]) for i in range(2)]
            ft = sd("ft", [128, DM])
            outt = sd("outt", [128, DM])
            bn12d = sd("bn12d", [128, 12])
            bn2d = sd("bn2d", [128, 2])
            m5 = sd("m5", [128, 1]); r5 = sd("r5", [128, 1])

            for g4 in range(4):
                K.op('qs', lambda e, g4=g4: e.dma_start(
                    out=wdq_s[:, g4 * 8 * DM:(g4 + 1) * 8 * DM]
                    .rearrange("p (k o) -> p k o", k=8),
                    in_=wdq_h[g4 * 8:(g4 + 1) * 8].rearrange("k p o -> p k o")),
                    reads=['wqd_h'], writes=['wdq_s'])

            for c in range(nch):
                p = c % 2
                qt, qtr = q2i[p], f'q2i{p}'
                xt3, xtr3 = x2j[p], f'x2j{p}'
                K.op('qs', lambda e, c=c, qt=qt: e.dma_start(out=qt[:], in_=q2_h[c]),
                     reads=['q2_h'], writes=[qtr])
                K.op('qs', lambda e, c=c, xt3=xt3: e.dma_start(out=xt3[:], in_=x2_h[c]),
                     reads=['x2_h'], writes=[xtr3])
                for ob in range(2):
                    pst_t = ps_a if ob == 0 else ps_b
                    pst_r = 'psa' if ob == 0 else 'psb'

                    def dnmm(e, ob=ob, pst_t=pst_t, qt=qt):
                        inst = None
                        for kc in range(32):
                            inst = e.matmul(pst_t[:],
                                            qt[:, kc * 128:(kc + 1) * 128],
                                            wdq_s[:, kc * DM + ob * 512: kc * DM + (ob + 1) * 512],
                                            start=(kc == 0), stop=(kc == 31))
                        return inst
                    K.op('pe', dnmm, reads=[qtr, 'wdq_s'], writes=[pst_r])
                    K.op('dve', lambda e, ob=ob, pst_t=pst_t, c=c: e.tensor_scalar(
                        ft[:, ob * 512:(ob + 1) * 512], pst_t[:], ds2_all[:, c:c + 1],
                        None, ALU.mult, ALU.bypass),
                        reads=[pst_r, 'ds2_all'], writes=['ft'])
                for i in range(2):
                    K.op('dve', lambda e, i=i: e.bn_stats(
                        bn12d[:, 6 * i:6 * i + 6], ft[:, i * 512:(i + 1) * 512]),
                        reads=['ft'], writes=['bn12d'])
                K.op('dve', lambda e: e.bn_aggr(bn2d[:], bn12d[:, 0:12]),
                     reads=['bn12d'], writes=['bn2d'])
                K.op('dve', lambda e: e.tensor_copy(m5[:], bn2d[:, 0:1]),
                     reads=['bn2d'], writes=['m5'])
                K.op('act', lambda e: e.activation(r5[:], bn2d[:, 1:2], AF.Ln, bias=np.float32(EPS)),
                     reads=['bn2d'], writes=['r5'])
                K.op('act', lambda e: e.activation(r5[:], r5[:], AF.Exp, scale=np.float32(-0.5)),
                     reads=['r5'], writes=['r5'])
                K.op('dve', lambda e: e.tensor_scalar(outt[:], ft[:], m5[:, 0:1],
                                                      r5[:, 0:1], ALU.subtract, ALU.mult),
                     reads=['ft', 'm5', 'r5'], writes=['outt'])
                K.op('dve', lambda e, xt3=xt3: e.tensor_tensor(outt[:], outt[:], xt3[:],
                                                               ALU.add),
                     reads=['outt', xtr3], writes=['outt'])
                K.op('qg', lambda e, c=c: e.dma_start(
                    out=out_h[c * L:(c + 1) * L, :], in_=outt[:]),
                    reads=['outt'], writes=['out_h'])

            K.barrier()

        blk = S0.enter_context(nc.Block())
        K.flush(blk)
    return nc


def _host_inputs(ins, nch=32):
    f32 = np.float32
    x = np.asarray(ins["L0_attn_x"], f32)
    Bsz, S, _ = x.shape
    Th = S // 2
    in_w = np.asarray(ins["L0_attn_in_w"], f32)
    shared = {
        "w_inT": np.ascontiguousarray(in_w.T),
        "in_b": np.ascontiguousarray(np.asarray(ins["L0_attn_in_b"], f32)[None, :]),
        "in_w_xc": np.ascontiguousarray(in_w[:DI]),
        "dt_wT": np.ascontiguousarray(np.asarray(ins["L0_attn_dt_w"], f32).T),
        "dt_b_row": np.ascontiguousarray(np.asarray(ins["L0_attn_dt_b"], f32)[None, :]),
        "xpw_kc": np.ascontiguousarray(
            np.asarray(ins["L0_attn_xproj_w"], f32)[20].reshape(16, 128).T),
        "xpb20": np.ascontiguousarray(
            np.asarray(ins["L0_attn_xproj_b"], f32)[20:21][None, :]),
        "convw_kc": np.ascontiguousarray(
            np.asarray(ins["L0_attn_conv_w"], f32)[:, 0, :].reshape(16, 128, 4)
            .transpose(1, 0, 2).reshape(128, 64)),
        "convb_kc": np.ascontiguousarray(
            np.asarray(ins["L0_attn_conv_b"], f32).reshape(16, 128).T),
        "a16": np.ascontiguousarray(np.asarray(ins["L0_attn_A"], f32)[:, None]),
        "bmT": np.ascontiguousarray(np.asarray(ins["L0_attn_Bm"], f32).T),
        "cmT": np.ascontiguousarray(np.asarray(ins["L0_attn_Cm"], f32).T),
        "bm_rows": np.ascontiguousarray(np.asarray(ins["L0_attn_Bm"], f32).T),
        "cm_rows": np.ascontiguousarray(np.asarray(ins["L0_attn_Cm"], f32).T),
        "out_wT": np.ascontiguousarray(np.asarray(ins["L0_attn_out_w"], f32).T),
        "out_b": np.ascontiguousarray(np.asarray(ins["L0_attn_out_b"], f32)[None, :]),
        "wgT": np.ascontiguousarray(np.asarray(ins["L0_ffn_w_gate"], f32).T),
        "wuT": np.ascontiguousarray(np.asarray(ins["L0_ffn_w_up"], f32).T),
        "wdT": np.ascontiguousarray(np.asarray(ins["L0_ffn_w_down"], f32).T),
        "ident_bf": np.eye(128, dtype=BF),
        "ident_f": np.eye(128, dtype=np.float32),
        "c_rgts": (np.arange(128)[:, None] > np.arange(128)[None, :]).astype(BF),
        "c_rlet": (np.arange(128)[:, None] <= np.arange(128)[None, :]).astype(BF),
        "c_mnegt": np.where(np.arange(128)[None, :] < np.arange(128)[:, None],
                            np.float32(-1e30), np.float32(0.0)).astype(BF),
    }
    zero = np.zeros((L, DM), f32)
    in_maps = []
    for core in range(8):
        b, h = core // 2, core % 2
        lo = h * Th
        m = dict(shared)
        m["x_in"] = np.ascontiguousarray(x[b, lo:lo + Th])
        m["xl"] = np.ascontiguousarray(x[b, lo - L:lo]) if h == 1 else zero
        m["xr"] = np.ascontiguousarray(x[b, Th:Th + L]) if h == 0 else zero
        sel = np.zeros((128, NS), f32)
        if h == 1:
            prev = core - 1
            for n in range(NS):
                sel[prev * NS + n, n] = 1.0
        m["selmat"] = sel
        in_maps.append(m)
    return in_maps


def kernel(**ins):
    nch = 32
    in_maps = _host_inputs(ins, nch)
    nc = build_program(nch)
    res = run_bass_kernel_spmd(nc, in_maps, core_ids=list(range(8)))
    x = np.asarray(ins["L0_attn_x"])
    Bsz, S, _ = x.shape
    Th = S // 2
    out = np.empty((Bsz, S, DM), np.float32)
    for core in range(8):
        b, h = core // 2, core % 2
        out[b, h * Th:(h + 1) * Th] = res.results[core]["out"]
    return out


# version 02
# speedup vs baseline: 1.1315x; 1.1315x over previous
import contextlib
import numpy as np
import ml_dtypes
import concourse.bass as bass
import concourse.mybir as mybir
from concourse.bass_utils import run_bass_kernel_spmd

F32 = mybir.dt.float32
BF16 = mybir.dt.bfloat16
AF = mybir.ActivationFunctionType
ALU = mybir.AluOpType
AX = mybir.AxisListType
BF = ml_dtypes.bfloat16

MAGIC = 12582912.0
EPS = 1e-5
L = 128
DM = 1024
DI = 2048
NS = 16
DF = 4096


class KB:
    ENGMAP = {
        'pe':   ('tensor', 'pe', 1),
        'dve':  ('vector', 'dve', 1),
        'act':  ('scalar', 'act', 1),
        'pool': ('gpsimd', 'pool', 1),
        'qs':   ('sync',   'qs', 16),
        'qg':   ('gpsimd', 'qg', 16),
        'cc':   ('gpsimd', 'cc', 1),
    }
    STREAMS = ('tensor', 'vector', 'scalar', 'gpsimd', 'sync')

    def __init__(self, nc, sems):
        self.nc = nc
        self.sems = sems
        self.cnt = {k: 0 for k in sems}
        self.ops = {s: [] for s in self.STREAMS}
        self.waited = {}
        self.lastw = {}
        self.readers = {}

    def _wait(self, stream, sem, val):
        key = (stream, sem)
        if val <= self.waited.get(key, 0):
            return
        self.waited[key] = val
        h = self.sems[sem]
        self.ops[stream].append(lambda e, h=h, v=val: e.wait_ge(h, v))

    def op(self, eng, fn, reads=(), writes=()):
        stream, sem, inc = self.ENGMAP[eng]
        deps = {}

        def add(tok):
            if tok is None:
                return
            s2, c2 = tok
            if eng == 'pe' and s2 == 'pe':
                return
            if c2 > deps.get(s2, 0):
                deps[s2] = c2

        for r in reads:
            add(self.lastw.get(r))
        for w in writes:
            add(self.lastw.get(w))
            for item in self.readers.get(w, {}).items():
                add(item)
        if inc == 16:
            self._wait(stream, sem, self.cnt[sem])
        for s2, c2 in sorted(deps.items()):
            self._wait(stream, s2, c2)
        self.cnt[sem] += inc
        mycnt = self.cnt[sem]
        h = self.sems[sem]

        def emit(e, fn=fn, h=h, inc=inc):
            inst = fn(e)
            inst.then_inc(h, inc)

        self.ops[stream].append(emit)
        tok = (sem, mycnt)
        for r in reads:
            self.readers.setdefault(r, {})[sem] = mycnt
        for w in writes:
            self.lastw[w] = tok
            self.readers[w] = {}
        return tok

    def barrier(self):
        snap = [(s, c) for s, c in self.cnt.items() if c > 0]
        for stream in self.STREAMS:
            for sem, c in snap:
                self._wait(stream, sem, c)

    def flush(self, blk):
        for stream in self.STREAMS:
            ops = self.ops[stream]
            if not ops:
                continue
            section = getattr(blk, {'tensor': 'tensor', 'vector': 'vector',
                                    'scalar': 'scalar', 'gpsimd': 'gpsimd',
                                    'sync': 'sync'}[stream])

            def body(e, ops=ops):
                for f in ops:
                    f(e)
            section(body)


def build_program(nch=32):
    T = nch * L
    NE = nch + 2
    nc = bass.Bass("TRN2", target_bir_lowering=False)

    def din(name, shape, dt=F32):
        return nc.dram_tensor(name, shape, dt, kind="ExternalInput").ap()

    def dint(name, shape, dt):
        return nc.dram_tensor(name, shape, dt, kind="Internal")

    x_in = din("x_in", [T, DM])
    xl = din("xl", [L, DM]); xr = din("xr", [L, DM])
    w_inT = din("w_inT", [DM, 2 * DI])
    in_b = din("in_b", [1, 2 * DI])
    in_w_xc = din("in_w_xc", [DI, DM])
    dt_wT = din("dt_wT", [DI, NS])
    dt_b_row = din("dt_b_row", [1, NS])
    xpw_kc = din("xpw_kc", [128, NS])
    xpb20 = din("xpb20", [1, 1])
    convw_kc = din("convw_kc", [128, NS * 4])
    convb_kc = din("convb_kc", [128, NS])
    a16 = din("a16", [NS, 1])
    bmT = din("bmT", [NS, DI]); cmT = din("cmT", [NS, DI])
    bm_rows = din("bm_rows", [NS, DI])
    cm_rows = din("cm_rows", [NS, DI])
    out_wT = din("out_wT", [DI, DM])
    out_b = din("out_b", [1, DM])
    wgT = din("wgT", [DM, DF]); wuT = din("wuT", [DM, DF]); wdT = din("wdT", [DF, DM])
    selmat = din("selmat", [128, NS])
    ident_bf = din("ident_bf", [128, 128], BF16)
    ident_f = din("ident_f", [128, 128], F32)
    c_rgts = din("c_rgts", [128, 128], BF16)
    c_rlet = din("c_rlet", [128, 128], BF16)
    c_mnegt = din("c_mnegt", [128, 128], BF16)

    out_h = nc.dram_tensor("out", [T, DM], F32, kind="ExternalOutput").ap()

    wres_h = dint("wres_h", [8, 128, DI], BF16).ap()
    wgq_h = dint("wgq_h", [8, 128, DF], BF16).ap()
    wuq_h = dint("wuq_h", [8, 128, DF], BF16).ap()
    wdq_h = dint("wdq_h", [32, 128, DM], BF16).ap()
    res_h = dint("res_h", [nch, L, DI], BF16).ap()
    yloc_h = dint("yloc_h", [nch, L, DI], BF16).ap()
    x2_h = dint("x2_h", [nch, L, DM], F32).ap()
    q2_h = dint("q2_h", [nch, 128, DF], BF16).ap()
    pl_h = dint("pl_h", [nch, NS, L], BF16).ap()
    ag_in_t = dint("ag_in", [NS, DI], F32)
    ag_out_t = dint("ag_out", [8 * NS, DI], F32)
    ag_in = ag_in_t.ap(); ag_out = ag_out_t.ap()

    sem_names = ['pe', 'dve', 'act', 'pool', 'qs', 'qg', 'cc']
    PSD_ALL = ['psd', 'psdd0', 'psdd1', 'psdd2', 'psdd3']

    with contextlib.ExitStack() as S0:
        sems = {n: S0.enter_context(nc.semaphore(n)) for n in sem_names}
        K = KB(nc, sems)

        ps_a = S0.enter_context(nc.psum_tensor("ps_a", [128, 512], F32))
        ps_b = S0.enter_context(nc.psum_tensor("ps_b", [128, 512], F32))
        ps_y = S0.enter_context(nc.psum_tensor("ps_y", [128, 2048], F32))
        ps_d = S0.enter_context(nc.psum_tensor("ps_d", [128, 512], F32))
        ps_t = S0.enter_context(nc.psum_tensor("ps_t", [128, 1024], BF16))

        sb = lambda n, sh, dt=F32: S0.enter_context(nc.sbuf_tensor(n, sh, dt))
        ones_row = sb("ones_row", [1, 128], BF16)
        ones_row_f = sb("ones_row_f", [1, 128])
        one11 = sb("one11", [1, 1], BF16)
        idb = sb("idb", [128, 128], BF16)
        idf = sb("idf", [128, 128], F32)
        rgts = sb("rgts", [128, 128], BF16)
        rlet = sb("rlet", [128, 128], BF16)
        mnegt = sb("mnegt", [128, 128], BF16)
        gcols = sb("gcols", [128, nch])
        ds2_all = sb("ds2_all", [128, nch], F32)
        cgcol = sb("cgcol", [128, 1]); cucol = sb("cucol", [128, 1])
        cdcol = sb("cdcol", [128, 1]); ngcol = sb("ngcol", [128, 1])
        h0c = sb("h0c", [NS, DI], BF16)
        spa_col = sb("spa_col", [NS, 1])
        dtcb = sb("dtcb", [NS, 1])
        gamc = sb("gamc", [1, 1])
        sQ = {}
        for wn in ('g', 'u', 'd'):
            sQ[wn] = sb(f"sw_{wn}", [1, 1])
            sQ[wn + 'col'] = sb(f"swc_{wn}", [128, 1])
            sQ[wn + 'c'] = sb(f"cw_{wn}", [1, 1])

        K.op('qs', lambda e: e.dma_start(out=idb[:], in_=ident_bf[:]), writes=['idb'])
        K.op('qs', lambda e: e.dma_start(out=idf[:], in_=ident_f[:]), writes=['idf'])
        K.op('qs', lambda e: e.dma_start(out=rgts[:], in_=c_rgts[:]), writes=['rgts'])
        K.op('qs', lambda e: e.dma_start(out=rlet[:], in_=c_rlet[:]), writes=['rlet'])
        K.op('qs', lambda e: e.dma_start(out=mnegt[:], in_=c_mnegt[:]), writes=['mnegt'])
        K.op('dve', lambda e: e.memset(ones_row[:], 1.0), writes=['ones_row'])
        K.op('dve', lambda e: e.memset(ones_row_f[:], 1.0), writes=['ones_row_f'])
        K.op('dve', lambda e: e.memset(one11[:], 1.0), writes=['one11'])

        def sp_chain(dst, src, region_src, region_dst, scratch, rs, P):
            t1, t2 = scratch
            r1, r2 = rs
            K.op('act', lambda e: e.activation(t1[:P, :], src, AF.Abs),
                 reads=[region_src], writes=[r1])
            K.op('act', lambda e: e.activation(t1[:P, :], t1[:P, :], AF.Exp, scale=np.float32(-1.0)),
                 reads=[r1], writes=[r1])
            K.op('act', lambda e: e.activation(t1[:P, :], t1[:P, :], AF.Ln, bias=np.float32(1.0)),
                 reads=[r1], writes=[r1])
            K.op('dve', lambda e: e.tensor_scalar(t2[:P, :], src, 0.0, None, ALU.max, ALU.bypass),
                 reads=[region_src], writes=[r2])
            K.op('dve', lambda e: e.tensor_tensor(dst, t1[:P, :], t2[:P, :], ALU.add),
                 reads=[r1, r2], writes=[region_dst])

        with contextlib.ExitStack() as SA:
            sa = lambda n, sh, dt=F32: SA.enter_context(nc.sbuf_tensor(n, sh, dt))
            wxc_bf = sa("wxc_bf", [128, 8 * DI], BF16)
            ebc = sa("ebc", [128, NS * DI], BF16)
            dtw_all = sa("dtw_all", [128, 4 * 8 * NS], BF16)
            xpw_all = sa("xpw_all", [128, 4 * 8], BF16)
            dbt = sa("dbt", [NS, DI])
            cmt_s = sa("cmt_s", [NS, DI])
            hstate = sa("hstate", [NS, DI])
            lam = sa("lam", [NS, 1])
            ones16 = sa("ones16", [NS, 128])
            hc = sa("hc", [NS, DI], BF16)
            inb_bf = sa("inb_bf", [1, 2 * DI], BF16)
            dtb_bf = sa("dtb_bf", [1, NS], BF16)
            xpb_bf = sa("xpb_bf", [1, 1], BF16)

            K.op('dve', lambda e: e.memset(hstate[:], 0.0), writes=['H'])
            K.op('dve', lambda e: e.memset(lam[:], 1.0), writes=['lam'])
            K.op('dve', lambda e: e.memset(ones16[:], 1.0), writes=['ones16'])

            with contextlib.ExitStack() as SP:
                sp = lambda n, sh, dt=F32: SP.enter_context(nc.sbuf_tensor(n, sh, dt))
                stg = [sp("stg0", [128, DI]), sp("stg1", [128, DI])]
                stg_b = sp("stg_b", [128, DI], BF16)
                inwxc_bf = [sp("inwxc0", [128, DM], BF16), sp("inwxc1", [128, DM], BF16)]
                row_a = sp("row_a", [1, 2 * DI])
                dtwt_bf = sp("dtwt_bf", [128, 16 * NS], BF16)
                cwf = sp("cwf", [128, NS * 4]); cbf = sp("cbf", [128, NS])
                cb_bf = sp("cb_bf", [128, NS], BF16)
                xpw_f = sp("xpw_f", [128, NS]); xpw_bf2 = sp("xpw_bf2", [128, NS], BF16)
                rk_bf = sp("rk_bf", [128, NS], BF16)
                rk1_bf = sp("rk1_bf", [128, 1], BF16)
                acol = sp("acol", [NS, 1]); acs = [sp("acs0", [NS, 1]), sp("acs1", [NS, 1])]

                for kc in range(8):
                    s = stg[kc % 2]; rs = f'stg{kc % 2}'
                    K.op('qs', lambda e, kc=kc, s=s: e.dma_start(
                        out=s[:], in_=w_inT[kc * 128:(kc + 1) * 128, 0:DI]), writes=[rs])
                    K.op('act', lambda e, kc=kc, s=s: e.activation(
                        wxc_bf[:, kc * DI:(kc + 1) * DI], s[:], AF.Copy),
                        reads=[rs], writes=['wxc_bf'])
                    K.op('qs', lambda e, kc=kc, s=s: e.dma_start(
                        out=s[:], in_=w_inT[kc * 128:(kc + 1) * 128, DI:2 * DI]), writes=[rs])
                    K.op('act', lambda e, kc=kc, s=s: e.activation(
                        stg_b[:], s[:], AF.Copy),
                        reads=[rs], writes=['stg_b'])
                    K.op('qg', lambda e, kc=kc: e.dma_start(
                        out=wres_h[kc], in_=stg_b[:]), reads=['stg_b'], writes=['wres_h'])
                K.op('qs', lambda e: e.dma_start(out=row_a[:, 0:2 * DI], in_=in_b[:]),
                     writes=['rowsc'])
                K.op('act', lambda e: e.activation(inb_bf[:], row_a[:, 0:2 * DI], AF.Copy),
                     reads=['rowsc'], writes=['inb_bf'])
                K.op('qs', lambda e: e.dma_start(out=row_a[:, 0:NS], in_=dt_b_row[:]),
                     writes=['rowsc'])
                K.op('act', lambda e: e.activation(dtb_bf[:], row_a[:, 0:NS], AF.Copy),
                     reads=['rowsc'], writes=['dtb_bf'])
                K.op('qs', lambda e: e.dma_start(out=row_a[:, 0:1], in_=xpb20[:]),
                     writes=['rowsc'])
                K.op('act', lambda e: e.activation(xpb_bf[:], row_a[:, 0:1], AF.Copy),
                     reads=['rowsc'], writes=['xpb_bf'])

                for kc in range(16):
                    s = stg[kc % 2]; rs = f'stg{kc % 2}'
                    K.op('qs', lambda e, kc=kc, s=s: e.dma_start(
                        out=s[:, 0:NS], in_=dt_wT[kc * 128:(kc + 1) * 128, :]), writes=[rs])
                    K.op('act', lambda e, kc=kc, s=s: e.activation(
                        dtwt_bf[:, kc * NS:(kc + 1) * NS], s[:, 0:NS], AF.Copy),
                        reads=[rs], writes=['dtwt_bf'])
                K.op('qs', lambda e: e.dma_start(out=cwf[:], in_=convw_kc[:]), writes=['cwf'])
                K.op('qs', lambda e: e.dma_start(out=cbf[:], in_=convb_kc[:]), writes=['cbf'])
                K.op('act', lambda e: e.activation(cb_bf[:], cbf[:], AF.Copy),
                     reads=['cbf'], writes=['cb_bf'])
                K.op('qs', lambda e: e.dma_start(out=xpw_f[:], in_=xpw_kc[:]), writes=['xpw_f'])
                K.op('act', lambda e: e.activation(xpw_bf2[:], xpw_f[:], AF.Copy),
                     reads=['xpw_f'], writes=['xpw_bf2'])

                for kc in range(16):
                    w = inwxc_bf[kc % 2]; wr = f'inwxc{kc % 2}'
                    s = stg[kc % 2]; rs = f'stg{kc % 2}'
                    K.op('qs', lambda e, kc=kc, s=s: e.dma_start(
                        out=s[:, 0:DM], in_=in_w_xc[kc * 128:(kc + 1) * 128, :]), writes=[rs])
                    K.op('act', lambda e, s=s, w=w: e.activation(
                        w[:], s[:, 0:DM], AF.Copy), reads=[rs], writes=[wr])
                    for k in range(4):
                        K.op('dve', lambda e, k=k, kc=kc: e.tensor_scalar(
                            rk_bf[:], dtwt_bf[:, kc * NS:(kc + 1) * NS],
                            cwf[:, kc * 4 + k:kc * 4 + k + 1], None, ALU.mult, ALU.bypass),
                            reads=['dtwt_bf', 'cwf'], writes=['rk_bf'])
                        K.op('dve', lambda e, k=k, kc=kc: e.tensor_scalar(
                            rk1_bf[:], xpw_bf2[:, kc:kc + 1],
                            cwf[:, kc * 4 + k:kc * 4 + k + 1], None, ALU.mult, ALU.bypass),
                            reads=['xpw_bf2', 'cwf'], writes=['rk1_bf'])

                        first = (kc == 0 and k == 0)
                        last = (kc == 15 and k == 3)

                        def dtwmm(e, k=k, kc=kc, w=w, first=first, last=last):
                            inst = None
                            for ic in range(8):
                                inst = e.matmul(
                                    ps_b[:, k * 128 + ic * NS: k * 128 + (ic + 1) * NS],
                                    w[:, ic * 128:(ic + 1) * 128],
                                    rk_bf[:], start=(first and ic == 0),
                                    stop=(last and ic == 7))
                            return inst
                        K.op('pe', dtwmm, reads=[wr, 'rk_bf'], writes=['psb'])

                        def xpwmm(e, k=k, kc=kc, w=w, first=first, last=last):
                            inst = None
                            for ic in range(8):
                                inst = e.matmul(
                                    ps_a[:, k * 8 + ic: k * 8 + ic + 1],
                                    w[:, ic * 128:(ic + 1) * 128],
                                    rk1_bf[:], start=(first and ic == 0),
                                    stop=(last and ic == 7))
                            return inst
                        K.op('pe', xpwmm, reads=[wr, 'rk1_bf'], writes=['psa'])
                K.op('act', lambda e: e.activation(
                    dtw_all[:], ps_b[:, 0:512], AF.Copy),
                    reads=['psb'], writes=['dtw_all'])
                K.op('act', lambda e: e.activation(
                    xpw_all[:], ps_a[:, 0:32], AF.Copy),
                    reads=['psa'], writes=['xpw_all'])

                for kc in range(16):
                    K.op('pe', lambda e, kc=kc: e.matmul(
                        ps_d[0:NS, 0:1], dtwt_bf[:, kc * NS:(kc + 1) * NS],
                        cb_bf[:, kc:kc + 1], start=(kc == 0), stop=False),
                        reads=['dtwt_bf', 'cb_bf'], writes=['psdd0'])
                    K.op('pe', lambda e, kc=kc: e.matmul(
                        ps_a[0:1, 0:1], xpw_bf2[:, kc:kc + 1],
                        cb_bf[:, kc:kc + 1], start=(kc == 0), stop=False),
                        reads=['xpw_bf2', 'cb_bf'], writes=['psa'])
                K.op('pe', lambda e: e.matmul(ps_d[0:NS, 0:1], dtb_bf[:],
                                              one11[:], start=False, stop=True),
                     reads=['dtb_bf', 'one11'], writes=['psdd0'])
                K.op('pe', lambda e: e.matmul(ps_a[0:1, 0:1], xpb_bf[:],
                                              one11[:], start=False, stop=True),
                     reads=['xpb_bf', 'one11'], writes=['psa'])
                K.op('dve', lambda e: e.tensor_copy(dtcb[:], ps_d[0:NS, 0:1]),
                     reads=['psdd0'], writes=['dtcb'])
                K.op('dve', lambda e: e.tensor_copy(gamc[:], ps_a[0:1, 0:1]),
                     reads=['psa'], writes=['gamc'])

                K.op('qs', lambda e: e.dma_start(out=acol[:], in_=a16[:]), writes=['acol'])
                sp_chain(spa_col[:], acol[:], 'acol', 'spa_col',
                         (acs[0], acs[1]), ('acs0', 'acs1'), NS)

                K.op('qs', lambda e: e.dma_start(out=dbt[:], in_=bmT[:]), writes=['dbt_raw'])
                K.op('act', lambda e: e.activation(dbt[:], dbt[:], AF.Abs),
                     reads=['dbt_raw'], writes=['dbt_abs'])
                K.op('qs', lambda e: e.dma_start(out=cmt_s[:], in_=cmT[:]), writes=['cmt'])

                dbt2 = stg[0]
                K.op('qs', lambda e: e.dma_start(out=dbt2[0:NS, :], in_=bmT[:]), writes=['stg0'])
                K.op('act', lambda e: e.activation(dbt[:], dbt[:], AF.Exp, scale=np.float32(-1.0)),
                     reads=['dbt_abs'], writes=['dbt_abs'])
                K.op('act', lambda e: e.activation(dbt[:], dbt[:], AF.Ln, bias=np.float32(1.0)),
                     reads=['dbt_abs'], writes=['dbt_abs'])
                K.op('dve', lambda e: e.tensor_scalar(dbt2[0:NS, :], dbt2[0:NS, :], 0.0, None,
                                                      ALU.max, ALU.bypass),
                     reads=['stg0'], writes=['stg0'])
                K.op('dve', lambda e: e.tensor_tensor(dbt[:], dbt[:], dbt2[0:NS, :], ALU.add),
                     reads=['dbt_abs', 'stg0'], writes=['dbt'])

                erow = row_a[0:1, 0:DI]
                erow2 = row_a[0:1, DI:2 * DI]
                erow_bf = sp("erow_bf", [1, DI], BF16)
                for n in range(NS):
                    K.op('qs', lambda e, n=n: e.dma_start(
                        out=erow, in_=bm_rows[n:n + 1, :]), writes=['rowsc'])
                    K.op('act', lambda e: e.activation(erow2, erow, AF.Abs),
                         reads=['rowsc'], writes=['rowsc'])
                    K.op('act', lambda e: e.activation(erow2, erow2, AF.Exp, scale=np.float32(-1.0)),
                         reads=['rowsc'], writes=['rowsc'])
                    K.op('act', lambda e: e.activation(erow2, erow2, AF.Ln, bias=np.float32(1.0)),
                         reads=['rowsc'], writes=['rowsc'])
                    K.op('dve', lambda e: e.tensor_scalar(erow, erow, 0.0, None,
                                                          ALU.max, ALU.bypass),
                         reads=['rowsc'], writes=['rowsc'])
                    K.op('dve', lambda e: e.tensor_tensor(erow, erow, erow2, ALU.add),
                         reads=['rowsc'], writes=['rowsc'])
                    K.op('qs', lambda e, n=n: e.dma_start(
                        out=erow2, in_=cm_rows[n:n + 1, :]), writes=['rowsc'])
                    K.op('dve', lambda e: e.tensor_tensor(erow, erow, erow2, ALU.mult),
                         reads=['rowsc'], writes=['rowsc'])
                    K.op('act', lambda e: e.activation(erow_bf[:], erow, AF.Copy),
                         reads=['rowsc'], writes=['erow_bf'])
                    for vb in range(4):
                        K.op('pe', lambda e, n=n, vb=vb: e.matmul(
                            ps_a[:, 0:512], ones_row[:],
                            erow_bf[:, vb * 512:(vb + 1) * 512], start=True, stop=True),
                            reads=['erow_bf', 'ones_row'], writes=['psa'])
                        K.op('act', lambda e, n=n, vb=vb: e.activation(
                            ebc[:, n * DI + vb * 512: n * DI + (vb + 1) * 512],
                            ps_a[:, 0:512], AF.Copy),
                            reads=['psa'], writes=['ebc'])

            K.barrier()
            with contextlib.ExitStack() as SW:
                sw = lambda n, sh, dt=F32: SW.enter_context(nc.sbuf_tensor(n, sh, dt))
                xin_t = sw("xin_t", [128, DM])
                h1tm = sw("h1tm", [128, DM], BF16)
                h1fm = [sw(f"h1fm{i}", [128, 8 * 128], BF16) for i in range(3)]
                gdt = [sw(f"gdt{i}", [NS, 512]) for i in range(3)]
                ggm = [sw(f"ggm{i}", [1, 512]) for i in range(3)]
                bn12 = sw("bn12", [128, 12])
                bn2t = sw("bn2t", [128, 2])
                mcol = sw("mcol", [128, 1]); rcol = sw("rcol", [128, 1])
                xtm = [sw(f"xtm{i}", [128, DI], BF16) for i in range(2)]
                xet = [sw(f"xet{i}", [128, DI], BF16) for i in range(2)]
                dnt = [sw(f"dnt{i}", [128, 128], BF16) for i in range(2)]
                rhsn = [sw(f"rhsn{i}", [128, 128], BF16) for i in range(2)]
                rest = sw("rest", [128, DI], BF16)
                ylt = sw("ylt", [128, DI], BF16)
                wres_s = [sw("wres_s0", [128, 4096], BF16)]; wres_s = wres_s * 2
                dtg = sw("dtg", [NS, 128])
                ggr = sw("ggr", [1, 128])
                sp1 = sw("sp1", [NS, 128]); sp2 = sw("sp2", [NS, 128])
                lnm = sw("lnm", [NS, 128])
                ltm = sw("ltm", [128, NS])
                cnm = sw("cnm", [NS, 128])
                pnm = sw("pnm", [NS, 128], BF16)
                plt = sw("plt", [NS, 128], BF16)
                e127 = sw("e127", [NS, 1])
                e2nm = sw("e2nm", [NS, 128], BF16)
                e2tm = sw("e2tm", [128, NS], BF16)
                vbt = sw("vbt", [NS, 512])
                grow = sw("grow", [1, 128]); gsig = sw("gsig", [1, 128])
                gcol_ps = sw("gcol_ps", [128, 1], BF16)

                def stats(src_tile, src_region, width):
                    nb = (width + 511) // 512
                    for i in range(nb):
                        lo = i * 512; hi = min(width, lo + 512)
                        K.op('dve', lambda e, lo=lo, hi=hi, i=i: e.bn_stats(
                            bn12[:, 6 * i:6 * i + 6], src_tile[:, lo:hi]),
                            reads=[src_region], writes=['bn12'])
                    K.op('dve', lambda e, nb=nb: e.bn_aggr(bn2t[:], bn12[:, 0:6 * nb]),
                         reads=['bn12'], writes=['bn2t'])
                    K.op('dve', lambda e: e.tensor_copy(mcol[:], bn2t[:, 0:1]),
                         reads=['bn2t'], writes=['mcol'])
                    K.op('act', lambda e: e.activation(rcol[:], bn2t[:, 1:2], AF.Ln, bias=np.float32(EPS)),
                         reads=['bn2t'], writes=['rcol'])
                    K.op('act', lambda e: e.activation(rcol[:], rcol[:], AF.Exp, scale=np.float32(-0.5)),
                         reads=['rcol'], writes=['rcol'])

                def wres_load(ob4):
                    buf = wres_s[ob4 % 2]
                    K.op('qs', lambda e, ob4=ob4, buf=buf: e.dma_start(
                        out=buf[:].rearrange("p (k o) -> p k o", k=8),
                        in_=wres_h[:, :, ob4 * 512:(ob4 + 1) * 512]
                        .rearrange("k p o -> p k o")),
                        reads=['wres_h'], writes=['wres_s0'])
                wres_load(0)

                for j in range(NE):
                    if j == 0:
                        src = xl
                    elif j == NE - 1:
                        src = xr
                    else:
                        src = x_in[(j - 1) * L: j * L, :]
                    K.op('qs', lambda e, src=src: e.dma_start(out=xin_t[:], in_=src),
                         writes=['xin_t'])
                    stats(xin_t, 'xin_t', DM)
                    K.op('dve', lambda e: e.tensor_scalar(
                        h1tm[:], xin_t[:], mcol[:, 0:1], rcol[:, 0:1],
                        ALU.subtract, ALU.mult),
                        reads=['xin_t', 'mcol', 'rcol'], writes=['h1tm'])
                    slot = j % 3
                    hf = h1fm[slot]; hfr = f'h1fm{slot}'

                    def t8(e, hf=hf):
                        inst = None
                        for ic in range(8):
                            inst = e.transpose(ps_t[:, ic * 128:(ic + 1) * 128],
                                               h1tm[:, ic * 128:(ic + 1) * 128], idb[:])
                        return inst
                    K.op('pe', t8, reads=['h1tm', 'idb'], writes=['pst'])
                    K.op('act', lambda e, hf=hf: e.activation(hf[:], ps_t[:], AF.Copy),
                         reads=['pst'], writes=[hfr])
                    def gmm(e, hf=hf):
                        inst = None
                        for k in range(4):
                            for ic in range(8):
                                inst = e.matmul(
                                    ps_d[0:NS, k * 128:(k + 1) * 128],
                                    dtw_all[:, (k * 8 + ic) * NS:(k * 8 + ic + 1) * NS],
                                    hf[:, ic * 128:(ic + 1) * 128],
                                    start=(ic == 0), stop=(ic == 7))
                        return inst
                    K.op('pe', gmm, reads=[hfr, 'dtw_all'], writes=PSD_ALL)
                    K.op('act', lambda e, g=gdt[slot]: e.activation(
                        g[:], ps_d[0:NS, :], AF.Copy), reads=PSD_ALL,
                        writes=[f'gdt{slot}'])

                    def gmm2(e, hf=hf):
                        inst = None
                        for k in range(4):
                            for ic in range(8):
                                inst = e.matmul(
                                    ps_d[0:1, k * 128:(k + 1) * 128],
                                    xpw_all[:, k * 8 + ic:k * 8 + ic + 1],
                                    hf[:, ic * 128:(ic + 1) * 128],
                                    start=(ic == 0), stop=(ic == 7))
                        return inst
                    K.op('pe', gmm2, reads=[hfr, 'xpw_all'], writes=PSD_ALL)
                    K.op('act', lambda e, g=ggm[slot]: e.activation(
                        g[:], ps_d[0:1, :], AF.Copy), reads=PSD_ALL,
                        writes=[f'ggm{slot}'])

                    if j < 2:
                        continue
                    c = j - 2
                    e_i = c + 1
                    sm1, s0, sp1s = (e_i - 1) % 3, e_i % 3, (e_i + 1) % 3
                    gm1, g0, gp1 = gdt[sm1], gdt[s0], gdt[sp1s]
                    rm1, r0, rp1 = f'gdt{sm1}', f'gdt{s0}', f'gdt{sp1s}'
                    hm0 = h1fm[s0]; hm0r = f'h1fm{s0}'
                    p = c % 2

                    K.op('dve', lambda e, g0=g0: e.tensor_copy(dtg[:, 1:128], g0[:, 0:127]),
                         reads=[r0], writes=['dtg'])
                    K.op('dve', lambda e, gm1=gm1: e.tensor_copy(dtg[:, 0:1], gm1[:, 127:128]),
                         reads=[rm1], writes=['dtg'])
                    K.op('dve', lambda e, g0=g0: e.tensor_tensor(
                        dtg[:], dtg[:], g0[:, 128:256], ALU.add), reads=['dtg', r0], writes=['dtg'])
                    K.op('dve', lambda e, g0=g0: e.tensor_tensor(
                        dtg[:, 0:127], dtg[:, 0:127], g0[:, 257:384], ALU.add),
                        reads=['dtg', r0], writes=['dtg'])
                    K.op('dve', lambda e, gp1=gp1: e.tensor_tensor(
                        dtg[:, 127:128], dtg[:, 127:128], gp1[:, 256:257], ALU.add),
                        reads=['dtg', rp1], writes=['dtg'])
                    K.op('dve', lambda e, g0=g0: e.tensor_tensor(
                        dtg[:, 0:126], dtg[:, 0:126], g0[:, 386:512], ALU.add),
                        reads=['dtg', r0], writes=['dtg'])
                    K.op('dve', lambda e, gp1=gp1: e.tensor_tensor(
                        dtg[:, 126:128], dtg[:, 126:128], gp1[:, 384:386], ALU.add),
                        reads=['dtg', rp1], writes=['dtg'])
                    gg0, ggm1, ggp1 = ggm[s0], ggm[sm1], ggm[sp1s]
                    q0, qm1, qp1 = f'ggm{s0}', f'ggm{sm1}', f'ggm{sp1s}'
                    K.op('dve', lambda e, gg0=gg0: e.tensor_copy(ggr[:, 1:128], gg0[:, 0:127]),
                         reads=[q0], writes=['ggr'])
                    K.op('dve', lambda e, ggm1=ggm1: e.tensor_copy(ggr[:, 0:1], ggm1[:, 127:128]),
                         reads=[qm1], writes=['ggr'])
                    K.op('dve', lambda e, gg0=gg0: e.tensor_tensor(
                        ggr[:], ggr[:], gg0[:, 128:256], ALU.add), reads=['ggr', q0], writes=['ggr'])
                    K.op('dve', lambda e, gg0=gg0: e.tensor_tensor(
                        ggr[:, 0:127], ggr[:, 0:127], gg0[:, 257:384], ALU.add),
                        reads=['ggr', q0], writes=['ggr'])
                    K.op('dve', lambda e, ggp1=ggp1: e.tensor_tensor(
                        ggr[:, 127:128], ggr[:, 127:128], ggp1[:, 256:257], ALU.add),
                        reads=['ggr', qp1], writes=['ggr'])
                    K.op('dve', lambda e, gg0=gg0: e.tensor_tensor(
                        ggr[:, 0:126], ggr[:, 0:126], gg0[:, 386:512], ALU.add),
                        reads=['ggr', q0], writes=['ggr'])
                    K.op('dve', lambda e, ggp1=ggp1: e.tensor_tensor(
                        ggr[:, 126:128], ggr[:, 126:128], ggp1[:, 384:386], ALU.add),
                        reads=['ggr', qp1], writes=['ggr'])

                    K.op('act', lambda e: e.activation(sp1[:], dtg[:], AF.Abs,
                                                       bias=dtcb[:, 0:1]),
                         reads=['dtg', 'dtcb'], writes=['sp1r'])
                    K.op('act', lambda e: e.activation(sp1[:], sp1[:], AF.Exp, scale=np.float32(-1.0)),
                         reads=['sp1r'], writes=['sp1r'])
                    K.op('act', lambda e: e.activation(sp1[:], sp1[:], AF.Ln, bias=np.float32(1.0)),
                         reads=['sp1r'], writes=['sp1r'])
                    K.op('dve', lambda e: e.tensor_scalar(sp2[:], dtg[:], dtcb[:, 0:1], 0.0,
                                                          ALU.add, ALU.max),
                         reads=['dtg', 'dtcb'], writes=['sp2r'])
                    K.op('dve', lambda e: e.tensor_tensor(sp2[:], sp1[:], sp2[:], ALU.add),
                         reads=['sp1r', 'sp2r'], writes=['sp2r'])
                    K.op('dve', lambda e: e.tensor_scalar(lnm[:], sp2[:], spa_col[:, 0:1],
                                                          -1.0, ALU.mult, ALU.mult),
                         reads=['sp2r', 'spa_col'], writes=['lnm'])

                    K.op('dve', lambda e: e.tensor_scalar(ggr[:], ggr[:], gamc[0:1, 0:1],
                                                          None, ALU.add, ALU.bypass),
                         reads=['ggr', 'gamc'], writes=['ggr'])
                    K.op('act', lambda e: e.activation(gsig[:], ggr[:], AF.Exp, scale=np.float32(-1.0)),
                         reads=['ggr'], writes=['gsig'])
                    K.op('dve', lambda e: e.tensor_scalar(gsig[:], gsig[:], 1.0, None,
                                                          ALU.add, ALU.bypass),
                         reads=['gsig'], writes=['gsig'])
                    K.op('dve', lambda e: e.reciprocal(gsig[:], gsig[:]),
                         reads=['gsig'], writes=['gsig'])
                    K.op('dve', lambda e: e.tensor_tensor(grow[:], ggr[:], gsig[:], ALU.mult),
                         reads=['ggr', 'gsig'], writes=['grow'])
                    K.op('pe', lambda e: e.transpose(ps_d[:, 0:1], grow[:], idf[0:1, 0:1]),
                         reads=['grow', 'idf'], writes=PSD_ALL)
                    K.op('act', lambda e, c=c: e.activation(gcols[:, c:c + 1],
                                                            ps_d[:, 0:1], AF.Copy),
                         reads=PSD_ALL, writes=['gcols'])

                    K.op('dve', lambda e: e.tensor_tensor_scan(
                        cnm[:], ones16[:], lnm[:], 0.0, ALU.mult, ALU.add),
                        reads=['ones16', 'lnm'], writes=['cnm'])
                    K.op('act', lambda e: e.activation(pnm[:], cnm[:], AF.Exp),
                         reads=['cnm'], writes=['pnm'])
                    K.op('act', lambda e: e.activation(e127[:], cnm[:, 127:128], AF.Exp),
                         reads=['cnm'], writes=['e127'])
                    K.op('dve', lambda e: e.tensor_scalar(plt[:], pnm[:], lam[:, 0:1],
                                                          None, ALU.mult, ALU.bypass),
                         reads=['pnm', 'lam'], writes=['plt'])
                    K.op('qg', lambda e, c=c: e.dma_start(out=pl_h[c], in_=plt[:]),
                         reads=['plt'], writes=['pl_h'])
                    K.op('dve', lambda e: e.tensor_scalar(lam[:], lam[:], e127[:, 0:1],
                                                          None, ALU.mult, ALU.bypass),
                         reads=['lam', 'e127'], writes=['lam'])
                    K.op('dve', lambda e: e.tensor_scalar(
                        sp1[:], cnm[:], cnm[:, 127:128], -1.0, ALU.subtract, ALU.mult),
                        reads=['cnm'], writes=['sp1r'])
                    K.op('act', lambda e: e.activation(e2nm[:], sp1[:], AF.Exp),
                         reads=['sp1r'], writes=['e2nm'])
                    K.op('pe', lambda e: e.transpose(ps_t[:, 0:NS], e2nm[:], idb[0:NS, 0:NS]),
                         reads=['e2nm', 'idb'], writes=['pst'])
                    K.op('act', lambda e: e.activation(e2tm[:], ps_t[:, 0:NS], AF.Copy),
                         reads=['pst'], writes=['e2tm'])
                    K.op('pe', lambda e: e.transpose(ps_d[:, 0:NS], lnm[:], idf[0:NS, 0:NS]),
                         reads=['lnm', 'idf'], writes=PSD_ALL)
                    K.op('dve', lambda e: e.tensor_copy(ltm[:], ps_d[:, 0:NS]),
                         reads=PSD_ALL, writes=['ltm'])

                    xt = xtm[p]; xtr = f'xtm{p}'
                    for ob in range(8):
                        pst_t = ps_a if ob % 2 == 0 else ps_b
                        pst_r = 'psa' if ob % 2 == 0 else 'psb'
                        ws = wres_s[ob % 2] if ob >= 4 else None
                        wsr = 'wres_s0'

                        def ipmm(e, ob=ob, pst_t=pst_t, hm0=hm0, ws=ws):
                            inst = None
                            for kc in range(8):
                                if ob < 4:
                                    rhs = wxc_bf[:, kc * DI + ob * 512: kc * DI + (ob + 1) * 512]
                                else:
                                    rhs = ws[:, kc * 512:(kc + 1) * 512]
                                inst = e.matmul(pst_t[:], hm0[:, kc * 128:(kc + 1) * 128],
                                                rhs, start=(kc == 0), stop=False)
                            bcol = inb_bf[0:1, ob * 512:(ob + 1) * 512]
                            inst = e.matmul(pst_t[:], ones_row[:], bcol,
                                            start=False, stop=True)
                            return inst
                        K.op('pe', ipmm,
                             reads=[hm0r, 'wxc_bf' if ob < 4 else wsr, 'inb_bf', 'ones_row'],
                             writes=[pst_r])
                        if ob >= 4:
                            nxt = (ob - 3) % 4
                            wres_load(nxt)
                        if ob < 4:
                            K.op('act', lambda e, ob=ob, pst_t=pst_t, xt=xt: e.activation(
                                xt[:, ob * 512:(ob + 1) * 512], pst_t[:], AF.Copy),
                                reads=[pst_r], writes=[xtr])
                        else:
                            K.op('act', lambda e, ob=ob, pst_t=pst_t: e.activation(
                                rest[:, (ob - 4) * 512:(ob - 3) * 512], pst_t[:], AF.Copy),
                                reads=[pst_r], writes=['rest'])
                    K.op('qg', lambda e, c=c: e.dma_start(out=res_h[c], in_=rest[:]),
                         reads=['rest'], writes=['res_h'])

                    for n in range(NS):
                        pn2 = n % 2
                        xe = xet[pn2]; xer = f'xet{pn2}'
                        dn = dnt[pn2]; dnr = f'dnt{pn2}'
                        rn = rhsn[pn2]; rnr = f'rhsn{pn2}'
                        slot_d = n % 4
                        psdr = f'psdd{slot_d}'
                        K.op('dve', lambda e, n=n, xe=xe, xt=xt: e.tensor_tensor(
                            xe[:], xt[:], ebc[:, n * DI:(n + 1) * DI], ALU.mult),
                            reads=[xtr, 'ebc'], writes=[xer])
                        K.op('dve', lambda e, n=n, rn=rn: e.tensor_scalar(
                            rn[:], rlet[:], ltm[:, n:n + 1], None, ALU.mult, ALU.bypass),
                            reads=['rlet', 'ltm'], writes=[rnr])

                        def dmm(e, slot_d=slot_d, rn=rn):
                            e.matmul(ps_d[:, slot_d * 128:(slot_d + 1) * 128],
                                     rgts[:], rn[:], start=True, stop=False)
                            return e.matmul(ps_d[:, slot_d * 128:(slot_d + 1) * 128],
                                            idb[:], mnegt[:], start=False, stop=True)
                        K.op('pe', dmm, reads=['rgts', rnr, 'idb', 'mnegt'], writes=[psdr])
                        K.op('act', lambda e, slot_d=slot_d, dn=dn: e.activation(
                            dn[:], ps_d[:, slot_d * 128:(slot_d + 1) * 128], AF.Exp),
                            reads=[psdr], writes=[dnr])

                        def ymm(e, n=n, dn=dn, xe=xe):
                            inst = None
                            for vb in range(4):
                                inst = e.matmul(ps_y[:, vb * 512:(vb + 1) * 512],
                                                dn[:], xe[:, vb * 512:(vb + 1) * 512],
                                                start=(n == 0), stop=False)
                            return inst
                        K.op('pe', ymm, reads=[dnr, xer], writes=['psy'])

                    K.op('dve', lambda e: e.tensor_tensor(hc[:], hstate[:], cmt_s[:], ALU.mult),
                         reads=['H', 'cmt'], writes=['hcr'])

                    def y2mm(e):
                        inst = None
                        for vb in range(4):
                            inst = e.matmul(ps_y[:, vb * 512:(vb + 1) * 512],
                                            pnm[:], hc[0:NS, vb * 512:(vb + 1) * 512],
                                            start=False, stop=True)
                        return inst
                    K.op('pe', y2mm, reads=['pnm', 'hcr'], writes=['psy'])
                    K.op('act', lambda e: e.activation(ylt[:], ps_y[:], AF.Copy),
                         reads=['psy'], writes=['ylt'])
                    K.op('qg', lambda e, c=c: e.dma_start(out=yloc_h[c], in_=ylt[:]),
                         reads=['ylt'], writes=['yloc_h'])

                    for vb in range(4):
                        K.op('pe', lambda e, vb=vb, xt=xt: e.matmul(
                            ps_b[0:NS, 0:512], e2tm[:],
                            xt[:, vb * 512:(vb + 1) * 512], start=True, stop=True),
                            reads=['e2tm', xtr], writes=['psb'])
                        K.op('dve', lambda e, vb=vb: e.tensor_tensor(
                            vbt[:], ps_b[0:NS, 0:512],
                            dbt[0:NS, vb * 512:(vb + 1) * 512], ALU.mult),
                            reads=['psb', 'dbt'], writes=['vbtr'])
                        K.op('dve', lambda e, vb=vb: e.scalar_tensor_tensor(
                            hstate[0:NS, vb * 512:(vb + 1) * 512],
                            hstate[0:NS, vb * 512:(vb + 1) * 512],
                            e127[:, 0:1], vbt[:], ALU.mult, ALU.add),
                            reads=['H', 'e127', 'vbtr'], writes=['H'])

            K.barrier()
            with contextlib.ExitStack() as SX:
                sx = lambda n, sh, dt=F32: SX.enter_context(nc.sbuf_tensor(n, sh, dt))
                K.op('qg', lambda e: e.dma_start(out=ag_in[:], in_=hstate[:]),
                     reads=['H'], writes=['ag_in_r'])
                K.op('cc', lambda e: e.collective_compute(
                    "AllGather", ALU.bypass, replica_groups=[list(range(8))],
                    ins=[ag_in_t.ap().opt()], outs=[ag_out_t.ap().opt()]),
                    reads=['ag_in_r'], writes=['ag_out_r'])
                ags = sx("ags", [128, DI])
                ags_bf = sx("ags_bf", [128, DI], BF16)
                selm = sx("selm", [128, NS], BF16)
                selm_f = sx("selm_f", [128, NS])
                h0t = sx("h0t", [NS, DI])
                K.op('qg', lambda e: e.dma_start(out=ags[:], in_=ag_out[:]),
                     reads=['ag_out_r'], writes=['ags'])
                K.op('act', lambda e: e.activation(ags_bf[:], ags[:], AF.Copy),
                     reads=['ags'], writes=['ags_bf'])
                K.op('qs', lambda e: e.dma_start(out=selm_f[:], in_=selmat[:]),
                     writes=['selm_f'])
                K.op('act', lambda e: e.activation(selm[:], selm_f[:], AF.Copy),
                     reads=['selm_f'], writes=['selm'])
                for vb in range(4):
                    K.op('pe', lambda e, vb=vb: e.matmul(
                        ps_a[0:NS, 0:512], selm[:],
                        ags_bf[:, vb * 512:(vb + 1) * 512], start=True, stop=True),
                        reads=['selm', 'ags_bf'], writes=['psa'])
                    K.op('dve', lambda e, vb=vb: e.tensor_copy(
                        h0t[0:NS, vb * 512:(vb + 1) * 512], ps_a[0:NS, 0:512]),
                        reads=['psa'], writes=['h0t'])
                K.op('dve', lambda e: e.tensor_tensor(h0c[:], h0t[:], cmt_s[:], ALU.mult),
                     reads=['h0t', 'cmt'], writes=['h0c'])
                K.barrier()

        with contextlib.ExitStack() as SB:
            sb1 = lambda n, sh, dt=F32: SB.enter_context(nc.sbuf_tensor(n, sh, dt))
            owt_bf = sb1("owt_bf", [128, 16 * DM], BF16)
            outb_bf = sb1("outb_bf", [1, DM], BF16)
            ylti = [sb1(f"ylti{i}", [128, DI], BF16) for i in range(2)]
            resi = [sb1(f"resi{i}", [128, DI], BF16) for i in range(2)]
            xi = [sb1(f"xi{i}", [128, DM]) for i in range(2)]
            plti = [sb1(f"plti{i}", [NS, 128], BF16) for i in range(2)]
            ygt = sb1("ygt", [128, DI], BF16)
            prefm = sb1("prefm", [128, DI], BF16)
            mo = sb1("mo", [128, DM])
            x2t = sb1("x2t", [128, DM])
            bn12b = sb1("bn12b", [128, 12])
            bn2b = sb1("bn2b", [128, 2])
            m3 = sb1("m3", [128, 1]); r3 = sb1("r3", [128, 1])
            stgo = [sb1("stgo0", [128, DM]), sb1("stgo1", [128, DM])]

            for kc in range(16):
                s = stgo[kc % 2]; rs = f'stgo{kc % 2}'
                K.op('qs', lambda e, kc=kc, s=s: e.dma_start(
                    out=s[:], in_=out_wT[kc * 128:(kc + 1) * 128, :]), writes=[rs])
                K.op('act', lambda e, kc=kc, s=s: e.activation(
                    owt_bf[:, kc * DM:(kc + 1) * DM], s[:], AF.Copy),
                    reads=[rs], writes=['owt_bf'])
            K.op('qs', lambda e: e.dma_start(out=stgo[0][0:1, :], in_=out_b[:]),
                 writes=['stgo0'])
            K.op('act', lambda e: e.activation(outb_bf[:], stgo[0][0:1, :], AF.Copy),
                 reads=['stgo0'], writes=['outb_bf'])

            qst = [sb1("qst0", [128, DF]), sb1("qst1", [128, DF])]
            qtt = sb1("qtt", [128, DF])
            qbf = sb1("qbf", [128, DF], BF16)
            accc = sb1("accc", [128, 1])
            onescol = sb1("onescol", [128, 1])
            qsteps = []

            def _qs_memset(wn):
                K.op('dve', lambda e: e.memset(accc[:], 0.0), writes=['accc'])

            def _qs_pass1(wn, wsrc, kc, odim):
                s = qst[kc % 2]; rs = f'qst{kc % 2}'
                K.op('qs', lambda e, kc=kc, s=s, wsrc=wsrc, odim=odim: e.dma_start(
                    out=s[:, 0:odim], in_=wsrc[kc * 128:(kc + 1) * 128, :]), writes=[rs])
                K.op('dve', lambda e, s=s, odim=odim: e.tensor_reduce(
                    qtt[:, 0:1], s[:, 0:odim], AX.X, ALU.add,
                    apply_absolute_value=True),
                    reads=[rs], writes=['qtt1'])
                K.op('dve', lambda e: e.tensor_tensor(accc[:], accc[:], qtt[:, 0:1],
                                                      ALU.add),
                     reads=['accc', 'qtt1'], writes=['accc'])

            def _qs_scale(wn, nel):
                K.op('pe', lambda e: e.matmul(ps_d[0:1, 0:1], accc[:], onescol[:],
                                              start=True, stop=True),
                     reads=['accc', 'onescol'], writes=PSD_ALL)
                K.op('dve', lambda e, wn=wn, nel=nel: e.tensor_scalar(
                    sQ[wn + 'c'][:], ps_d[0:1, 0:1], 1.0 / nel, 1e-5, ALU.mult, ALU.max),
                    reads=PSD_ALL, writes=[f'cw{wn}'])
                K.op('dve', lambda e, wn=wn: e.reciprocal(sQ[wn][:], sQ[wn + 'c'][:]),
                     reads=[f'cw{wn}'], writes=[f'sw{wn}'])
                K.op('pe', lambda e, wn=wn: e.matmul(
                    ps_d[:, 0:1], ones_row_f[:], sQ[wn][:], start=True, stop=True),
                    reads=[f'sw{wn}', 'ones_row_f'], writes=PSD_ALL)
                K.op('dve', lambda e, wn=wn: e.tensor_copy(
                    sQ[wn + 'col'][:], ps_d[:, 0:1]),
                    reads=PSD_ALL, writes=[f'swc{wn}'])

            def _qs_pass2(wn, wsrc, wdst, kc, odim):
                s = qst[kc % 2]; rs = f'qst{kc % 2}'
                K.op('qs', lambda e, kc=kc, s=s, wsrc=wsrc, odim=odim: e.dma_start(
                    out=s[:, 0:odim], in_=wsrc[kc * 128:(kc + 1) * 128, :]), writes=[rs])
                K.op('dve', lambda e, s=s, wn=wn, odim=odim: e.tensor_scalar(
                    qtt[:, 0:odim], s[:, 0:odim], sQ[wn + 'col'][:, 0:1], MAGIC,
                    ALU.mult, ALU.add),
                    reads=[rs, f'swc{wn}'], writes=['qtt1'])
                K.op('dve', lambda e, odim=odim: e.tensor_scalar(
                    qtt[:, 0:odim], qtt[:, 0:odim], MAGIC, -1.0,
                    ALU.subtract, ALU.max),
                    reads=['qtt1'], writes=['qtt1'])
                K.op('dve', lambda e, odim=odim: e.tensor_scalar(
                    qbf[:, 0:odim], qtt[:, 0:odim], 1.0, None, ALU.min, ALU.bypass),
                    reads=['qtt1'], writes=['qbf'])
                K.op('qg', lambda e, kc=kc, wdst=wdst, odim=odim: e.dma_start(
                    out=wdst[kc], in_=qbf[:, 0:odim]),
                    reads=['qbf'], writes=[f'wq{wn}_h'])

            K.op('dve', lambda e: e.memset(onescol[:], 1.0), writes=['onescol'])
            import functools
            for wn, wsrc, wdst, nkc, odim in (
                    ('g', wgT, wgq_h, 8, DF), ('u', wuT, wuq_h, 8, DF),
                    ('d', wdT, wdq_h, 32, DM)):
                qsteps.append(functools.partial(_qs_memset, wn))
                for kc in range(nkc):
                    qsteps.append(functools.partial(_qs_pass1, wn, wsrc, kc, odim))
                qsteps.append(functools.partial(_qs_scale, wn, float(nkc * 128 * odim)))
                for kc in range(nkc):
                    qsteps.append(functools.partial(_qs_pass2, wn, wsrc, wdst, kc, odim))

            for c in range(nch):
                p = c % 2
                yt, ytr = ylti[p], f'ylti{p}'
                rt, rtr = resi[p], f'resi{p}'
                xt2, xtr2 = xi[p], f'xi{p}'
                pt, ptr = plti[p], f'plti{p}'
                K.op('qs', lambda e, c=c, yt=yt: e.dma_start(out=yt[:], in_=yloc_h[c]),
                     reads=['yloc_h'], writes=[ytr])
                K.op('qs', lambda e, c=c, rt=rt: e.dma_start(out=rt[:], in_=res_h[c]),
                     reads=['res_h'], writes=[rtr])
                K.op('qs', lambda e, c=c, xt2=xt2: e.dma_start(
                    out=xt2[:], in_=x_in[c * L:(c + 1) * L, :]), writes=[xtr2])
                K.op('qs', lambda e, c=c, pt=pt: e.dma_start(out=pt[:], in_=pl_h[c]),
                     reads=['pl_h'], writes=[ptr])

                def cmm(e, pt=pt, yt=yt):
                    inst = None
                    for vb in range(4):
                        e.matmul(ps_y[:, vb * 512:(vb + 1) * 512], pt[:],
                                 h0c[0:NS, vb * 512:(vb + 1) * 512],
                                 start=True, stop=False)
                        inst = e.matmul(ps_y[:, vb * 512:(vb + 1) * 512], idb[:],
                                        yt[:, vb * 512:(vb + 1) * 512],
                                        start=False, stop=True)
                    return inst
                K.op('pe', cmm, reads=[ptr, 'h0c', 'idb', ytr], writes=['psy'])
                K.op('dve', lambda e, c=c: e.tensor_scalar(
                    ygt[:], ps_y[:], gcols[:, c:c + 1], None, ALU.mult, ALU.bypass),
                    reads=['psy', 'gcols'], writes=['ygt'])
                K.op('dve', lambda e, rt=rt: e.tensor_tensor(ygt[:], ygt[:], rt[:], ALU.add),
                     reads=['ygt', rtr], writes=['ygt'])
                for rnd in range(2):
                    def tr8(e, rnd=rnd):
                        inst = None
                        for i in range(8):
                            blk_i = rnd * 8 + i
                            inst = e.transpose(ps_t[:, i * 128:(i + 1) * 128],
                                               ygt[:, blk_i * 128:(blk_i + 1) * 128], idb[:])
                        return inst
                    K.op('pe', tr8, reads=['ygt', 'idb'], writes=['pst'])
                    K.op('act', lambda e, rnd=rnd: e.activation(
                        prefm[:, rnd * 1024:(rnd + 1) * 1024], ps_t[:], AF.Copy),
                        reads=['pst'], writes=['prefm'])
                for ob in range(2):
                    pst_t = ps_a if ob == 0 else ps_b
                    pst_r = 'psa' if ob == 0 else 'psb'

                    def opmm(e, ob=ob, pst_t=pst_t):
                        inst = None
                        for kc in range(16):
                            inst = e.matmul(pst_t[:],
                                            prefm[:, kc * 128:(kc + 1) * 128],
                                            owt_bf[:, kc * DM + ob * 512: kc * DM + (ob + 1) * 512],
                                            start=(kc == 0), stop=False)
                        inst = e.matmul(pst_t[:], ones_row[:],
                                        outb_bf[0:1, ob * 512:(ob + 1) * 512],
                                        start=False, stop=True)
                        return inst
                    K.op('pe', opmm, reads=['prefm', 'owt_bf', 'outb_bf', 'ones_row'],
                         writes=[pst_r])
                    K.op('act', lambda e, ob=ob, pst_t=pst_t: e.activation(
                        mo[:, ob * 512:(ob + 1) * 512], pst_t[:], AF.Copy),
                        reads=[pst_r], writes=['mo'])
                for i in range(2):
                    K.op('dve', lambda e, i=i: e.bn_stats(
                        bn12b[:, 6 * i:6 * i + 6], mo[:, i * 512:(i + 1) * 512]),
                        reads=['mo'], writes=['bn12b'])
                K.op('dve', lambda e: e.bn_aggr(bn2b[:], bn12b[:, 0:12]),
                     reads=['bn12b'], writes=['bn2b'])
                K.op('dve', lambda e: e.tensor_copy(m3[:], bn2b[:, 0:1]),
                     reads=['bn2b'], writes=['m3'])
                K.op('act', lambda e: e.activation(r3[:], bn2b[:, 1:2], AF.Ln, bias=np.float32(EPS)),
                     reads=['bn2b'], writes=['r3'])
                K.op('act', lambda e: e.activation(r3[:], r3[:], AF.Exp, scale=np.float32(-0.5)),
                     reads=['r3'], writes=['r3'])
                K.op('dve', lambda e: e.tensor_scalar(x2t[:], mo[:], m3[:, 0:1],
                                                      r3[:, 0:1], ALU.subtract, ALU.mult),
                     reads=['mo', 'm3', 'r3'], writes=['x2t'])
                K.op('dve', lambda e, xt2=xt2: e.tensor_tensor(x2t[:], x2t[:], xt2[:], ALU.add),
                     reads=['x2t', xtr2], writes=['x2t'])
                K.op('qg', lambda e, c=c: e.dma_start(out=x2_h[c], in_=x2t[:]),
                     reads=['x2t'], writes=['x2_h'])
                for _ in range(3):
                    if qsteps:
                        qsteps.pop(0)()
            while qsteps:
                qsteps.pop(0)()
            K.barrier()

        with contextlib.ExitStack() as SF:
            sf = lambda n, sh, dt=F32: SF.enter_context(nc.sbuf_tensor(n, sh, dt))
            wgq_s = sf("wgq_s", [128, 8 * DF], BF16)
            wuq_s = sf("wuq_s", [128, 8 * DF], BF16)
            x2i = [sf(f"x2i{i}", [128, DM]) for i in range(2)]
            q1tm = sf("q1tm", [128, DM], BF16)
            q1fm = sf("q1fm", [128, DM], BF16)
            gatet = sf("gatet", [128, DF], BF16)
            upt = sf("upt", [128, DF], BF16)
            gut = sf("gut", [128, DF], BF16)
            t5h = sf("t5h", [128, DI])
            q2tm = sf("q2tm", [128, DF], BF16)
            q2fm = sf("q2fm", [128, DF], BF16)
            bn12c = sf("bn12c", [128, 12])
            bn2c = sf("bn2c", [128, 2])
            m4 = sf("m4", [128, 1]); r4 = sf("r4", [128, 1])
            mxc = sf("mxc", [128, 1]); mnc = sf("mnc", [128, 1])
            s4 = sf("s4", [128, 1]); u4 = sf("u4", [128, 1])
            gsc = sf("gsc", [128, 1]); usc = sf("usc", [128, 1])
            s2c = sf("s2c", [128, 1]); is4 = sf("is4", [128, 1])
            mg = sf("mg", [128, 1])

            K.op('qs', lambda e: e.dma_start(
                out=wgq_s[:].rearrange("p (k o) -> p k o", k=8),
                in_=wgq_h.rearrange("k p o -> p k o")),
                reads=['wqg_h'], writes=['wgq_s'])
            K.op('qs', lambda e: e.dma_start(
                out=wuq_s[:].rearrange("p (k o) -> p k o", k=8),
                in_=wuq_h.rearrange("k p o -> p k o")),
                reads=['wqu_h'], writes=['wuq_s'])
            for wn, col in (('g', cgcol), ('u', cucol), ('d', cdcol)):
                K.op('pe', lambda e, wn=wn: e.matmul(
                    ps_d[:, 0:1], ones_row_f[:], sQ[wn + 'c'][:], start=True, stop=True),
                    reads=[f'cw{wn}', 'ones_row_f'], writes=PSD_ALL)
                K.op('dve', lambda e, wn=wn, col=col: e.tensor_copy(
                    col[:], ps_d[:, 0:1]),
                    reads=PSD_ALL, writes=[f'{wn}ccol'])

            for c in range(nch):
                p = c % 2
                xt2, xtr2 = x2i[p], f'x2i{p}'
                K.op('qs', lambda e, c=c, xt2=xt2: e.dma_start(out=xt2[:], in_=x2_h[c]),
                     reads=['x2_h'], writes=[xtr2])
                for i in range(2):
                    K.op('dve', lambda e, i=i, xt2=xt2: e.bn_stats(
                        bn12c[:, 6 * i:6 * i + 6], xt2[:, i * 512:(i + 1) * 512]),
                        reads=[xtr2], writes=['bn12c'])
                K.op('dve', lambda e: e.bn_aggr(bn2c[:], bn12c[:, 0:12]),
                     reads=['bn12c'], writes=['bn2c'])
                K.op('dve', lambda e: e.tensor_copy(m4[:], bn2c[:, 0:1]),
                     reads=['bn2c'], writes=['m4'])
                K.op('act', lambda e: e.activation(r4[:], bn2c[:, 1:2], AF.Ln, bias=np.float32(EPS)),
                     reads=['bn2c'], writes=['r4'])
                K.op('act', lambda e: e.activation(r4[:], r4[:], AF.Exp, scale=np.float32(-0.5)),
                     reads=['r4'], writes=['r4'])
                K.op('dve', lambda e, xt2=xt2: e.tensor_reduce(
                    mxc[:], xt2[:], AX.X, ALU.max), reads=[xtr2], writes=['mxc'])
                K.op('dve', lambda e, xt2=xt2: e.tensor_reduce(
                    mnc[:], xt2[:], AX.X, ALU.min), reads=[xtr2], writes=['mnc'])
                K.op('dve', lambda e: e.tensor_scalar(mxc[:], mxc[:], m4[:, 0:1], None,
                                                      ALU.subtract, ALU.bypass),
                     reads=['mxc', 'm4'], writes=['mxc'])
                K.op('dve', lambda e: e.tensor_scalar(mnc[:], mnc[:], m4[:, 0:1], -1.0,
                                                      ALU.subtract, ALU.mult),
                     reads=['mnc', 'm4'], writes=['mnc'])
                K.op('dve', lambda e: e.tensor_tensor(mxc[:], mxc[:], mnc[:], ALU.max),
                     reads=['mxc', 'mnc'], writes=['mxc'])
                K.op('dve', lambda e: e.tensor_scalar(s4[:], mxc[:], r4[:, 0:1], 1e-5,
                                                      ALU.mult, ALU.max),
                     reads=['mxc', 'r4'], writes=['s4'])
                K.op('dve', lambda e: e.reciprocal(s4[:], s4[:]),
                     reads=['s4'], writes=['s4'])
                K.op('dve', lambda e: e.tensor_scalar(s4[:], s4[:], 127.0, None,
                                                      ALU.mult, ALU.bypass),
                     reads=['s4'], writes=['s4'])
                K.op('dve', lambda e: e.tensor_tensor(u4[:], r4[:], s4[:], ALU.mult),
                     reads=['r4', 's4'], writes=['u4'])
                K.op('dve', lambda e, xt2=xt2: e.tensor_scalar(
                    t5h[:, 0:DM], xt2[:], m4[:, 0:1], u4[:, 0:1], ALU.subtract, ALU.mult),
                    reads=[xtr2, 'm4', 'u4'], writes=['t5h'])
                K.op('dve', lambda e: e.tensor_scalar(q1tm[:], t5h[:, 0:DM], MAGIC, MAGIC,
                                                      ALU.add, ALU.subtract),
                     reads=['t5h'], writes=['q1tm'])

                def q1t(e):
                    inst = None
                    for ic in range(8):
                        inst = e.transpose(ps_t[:, ic * 128:(ic + 1) * 128],
                                           q1tm[:, ic * 128:(ic + 1) * 128], idb[:])
                    return inst
                K.op('pe', q1t, reads=['q1tm', 'idb'], writes=['pst'])
                K.op('act', lambda e: e.activation(q1fm[:], ps_t[:, 0:DM], AF.Copy),
                     reads=['pst'], writes=['q1fm'])
                K.op('dve', lambda e: e.reciprocal(is4[:], s4[:]),
                     reads=['s4'], writes=['is4'])
                K.op('dve', lambda e: e.tensor_tensor(gsc[:], is4[:], cgcol[:], ALU.mult),
                     reads=['is4', 'gccol'], writes=['gsc'])
                K.op('dve', lambda e: e.tensor_tensor(usc[:], is4[:], cucol[:], ALU.mult),
                     reads=['is4', 'uccol'], writes=['usc'])
                for fb in range(8):
                    def gamm(e, fb=fb):
                        inst = None
                        for kc in range(8):
                            inst = e.matmul(ps_a[:],
                                            q1fm[:, kc * 128:(kc + 1) * 128],
                                            wgq_s[:, kc * DF + fb * 512: kc * DF + (fb + 1) * 512],
                                            start=(kc == 0), stop=(kc == 7))
                        return inst
                    K.op('pe', gamm, reads=['q1fm', 'wgq_s'], writes=['psa'])
                    K.op('act', lambda e, fb=fb: e.activation(
                        gatet[:, fb * 512:(fb + 1) * 512], ps_a[:], AF.Silu,
                        scale=gsc[:, 0:1]),
                        reads=['psa', 'gsc'], writes=['gatet'])

                    def upmm(e, fb=fb):
                        inst = None
                        for kc in range(8):
                            inst = e.matmul(ps_b[:],
                                            q1fm[:, kc * 128:(kc + 1) * 128],
                                            wuq_s[:, kc * DF + fb * 512: kc * DF + (fb + 1) * 512],
                                            start=(kc == 0), stop=(kc == 7))
                        return inst
                    K.op('pe', upmm, reads=['q1fm', 'wuq_s'], writes=['psb'])
                    K.op('dve', lambda e, fb=fb: e.tensor_scalar(
                        upt[:, fb * 512:(fb + 1) * 512], ps_b[:], usc[:, 0:1], None,
                        ALU.mult, ALU.bypass),
                        reads=['psb', 'usc'], writes=['upt'])
                K.op('dve', lambda e: e.tensor_tensor(gut[:], gatet[:], upt[:], ALU.mult),
                     reads=['gatet', 'upt'], writes=['gut'])
                K.op('dve', lambda e: e.tensor_reduce(mg[:], gut[:], AX.X, ALU.max,
                                                      apply_absolute_value=True),
                     reads=['gut'], writes=['mg'])
                K.op('dve', lambda e: e.tensor_scalar(s2c[:], mg[:], 1e-5, None,
                                                      ALU.max, ALU.bypass),
                     reads=['mg'], writes=['s2c'])
                K.op('dve', lambda e: e.reciprocal(s2c[:], s2c[:]),
                     reads=['s2c'], writes=['s2c'])
                K.op('dve', lambda e: e.tensor_scalar(s2c[:], s2c[:], 127.0, None,
                                                      ALU.mult, ALU.bypass),
                     reads=['s2c'], writes=['s2c'])
                K.op('dve', lambda e, c=c: e.reciprocal(mg[:], s2c[:]),
                     reads=['s2c'], writes=['mg'])
                K.op('dve', lambda e, c=c: e.tensor_tensor(
                    ds2_all[:, c:c + 1], mg[:], cdcol[:], ALU.mult),
                    reads=['mg', 'dccol'], writes=['ds2_all'])
                for hh in range(2):
                    K.op('dve', lambda e, hh=hh: e.tensor_scalar(
                        t5h[:], gut[:, hh * DI:(hh + 1) * DI], s2c[:, 0:1], MAGIC,
                        ALU.mult, ALU.add),
                        reads=['gut', 's2c'], writes=['t5h'])
                    K.op('dve', lambda e, hh=hh: e.tensor_scalar(
                        q2tm[:, hh * DI:(hh + 1) * DI], t5h[:], MAGIC, None,
                        ALU.subtract, ALU.bypass),
                        reads=['t5h'], writes=['q2tm'])
                for rnd in range(4):
                    def q2t(e, rnd=rnd):
                        inst = None
                        for i in range(8):
                            blk_i = rnd * 8 + i
                            inst = e.transpose(ps_t[:, i * 128:(i + 1) * 128],
                                               q2tm[:, blk_i * 128:(blk_i + 1) * 128], idb[:])
                        return inst
                    K.op('pe', q2t, reads=['q2tm', 'idb'], writes=['pst'])
                    K.op('act', lambda e, rnd=rnd: e.activation(
                        q2fm[:, rnd * 1024:(rnd + 1) * 1024], ps_t[:], AF.Copy),
                        reads=['pst'], writes=['q2fm'])
                K.op('qg', lambda e, c=c: e.dma_start(out=q2_h[c], in_=q2fm[:]),
                     reads=['q2fm'], writes=['q2_h'])
            K.barrier()

        with contextlib.ExitStack() as SD:
            sd = lambda n, sh, dt=F32: SD.enter_context(nc.sbuf_tensor(n, sh, dt))
            wdq_s = sd("wdq_s", [128, 32 * DM], BF16)
            q2i = [sd(f"q2i{i}", [128, DF], BF16) for i in range(2)]
            x2j = [sd(f"x2j{i}", [128, DM]) for i in range(2)]
            ft = sd("ft", [128, DM])
            outt = sd("outt", [128, DM])
            bn12d = sd("bn12d", [128, 12])
            bn2d = sd("bn2d", [128, 2])
            m5 = sd("m5", [128, 1]); r5 = sd("r5", [128, 1])

            for g4 in range(4):
                K.op('qs', lambda e, g4=g4: e.dma_start(
                    out=wdq_s[:, g4 * 8 * DM:(g4 + 1) * 8 * DM]
                    .rearrange("p (k o) -> p k o", k=8),
                    in_=wdq_h[g4 * 8:(g4 + 1) * 8].rearrange("k p o -> p k o")),
                    reads=['wqd_h'], writes=['wdq_s'])

            for c in range(nch):
                p = c % 2
                qt, qtr = q2i[p], f'q2i{p}'
                xt3, xtr3 = x2j[p], f'x2j{p}'
                K.op('qs', lambda e, c=c, qt=qt: e.dma_start(out=qt[:], in_=q2_h[c]),
                     reads=['q2_h'], writes=[qtr])
                K.op('qs', lambda e, c=c, xt3=xt3: e.dma_start(out=xt3[:], in_=x2_h[c]),
                     reads=['x2_h'], writes=[xtr3])
                for ob in range(2):
                    pst_t = ps_a if ob == 0 else ps_b
                    pst_r = 'psa' if ob == 0 else 'psb'

                    def dnmm(e, ob=ob, pst_t=pst_t, qt=qt):
                        inst = None
                        for kc in range(32):
                            inst = e.matmul(pst_t[:],
                                            qt[:, kc * 128:(kc + 1) * 128],
                                            wdq_s[:, kc * DM + ob * 512: kc * DM + (ob + 1) * 512],
                                            start=(kc == 0), stop=(kc == 31))
                        return inst
                    K.op('pe', dnmm, reads=[qtr, 'wdq_s'], writes=[pst_r])
                    K.op('dve', lambda e, ob=ob, pst_t=pst_t, c=c: e.tensor_scalar(
                        ft[:, ob * 512:(ob + 1) * 512], pst_t[:], ds2_all[:, c:c + 1],
                        None, ALU.mult, ALU.bypass),
                        reads=[pst_r, 'ds2_all'], writes=['ft'])
                for i in range(2):
                    K.op('dve', lambda e, i=i: e.bn_stats(
                        bn12d[:, 6 * i:6 * i + 6], ft[:, i * 512:(i + 1) * 512]),
                        reads=['ft'], writes=['bn12d'])
                K.op('dve', lambda e: e.bn_aggr(bn2d[:], bn12d[:, 0:12]),
                     reads=['bn12d'], writes=['bn2d'])
                K.op('dve', lambda e: e.tensor_copy(m5[:], bn2d[:, 0:1]),
                     reads=['bn2d'], writes=['m5'])
                K.op('act', lambda e: e.activation(r5[:], bn2d[:, 1:2], AF.Ln, bias=np.float32(EPS)),
                     reads=['bn2d'], writes=['r5'])
                K.op('act', lambda e: e.activation(r5[:], r5[:], AF.Exp, scale=np.float32(-0.5)),
                     reads=['r5'], writes=['r5'])
                K.op('dve', lambda e: e.tensor_scalar(outt[:], ft[:], m5[:, 0:1],
                                                      r5[:, 0:1], ALU.subtract, ALU.mult),
                     reads=['ft', 'm5', 'r5'], writes=['outt'])
                K.op('dve', lambda e, xt3=xt3: e.tensor_tensor(outt[:], outt[:], xt3[:],
                                                               ALU.add),
                     reads=['outt', xtr3], writes=['outt'])
                K.op('qg', lambda e, c=c: e.dma_start(
                    out=out_h[c * L:(c + 1) * L, :], in_=outt[:]),
                    reads=['outt'], writes=['out_h'])

            K.barrier()

        blk = S0.enter_context(nc.Block())
        K.flush(blk)
    return nc


def _host_inputs(ins, nch=32):
    f32 = np.float32
    x = np.asarray(ins["L0_attn_x"], f32)
    Bsz, S, _ = x.shape
    Th = S // 2
    in_w = np.asarray(ins["L0_attn_in_w"], f32)
    shared = {
        "w_inT": np.ascontiguousarray(in_w.T),
        "in_b": np.ascontiguousarray(np.asarray(ins["L0_attn_in_b"], f32)[None, :]),
        "in_w_xc": np.ascontiguousarray(in_w[:DI]),
        "dt_wT": np.ascontiguousarray(np.asarray(ins["L0_attn_dt_w"], f32).T),
        "dt_b_row": np.ascontiguousarray(np.asarray(ins["L0_attn_dt_b"], f32)[None, :]),
        "xpw_kc": np.ascontiguousarray(
            np.asarray(ins["L0_attn_xproj_w"], f32)[20].reshape(16, 128).T),
        "xpb20": np.ascontiguousarray(
            np.asarray(ins["L0_attn_xproj_b"], f32)[20:21][None, :]),
        "convw_kc": np.ascontiguousarray(
            np.asarray(ins["L0_attn_conv_w"], f32)[:, 0, :].reshape(16, 128, 4)
            .transpose(1, 0, 2).reshape(128, 64)),
        "convb_kc": np.ascontiguousarray(
            np.asarray(ins["L0_attn_conv_b"], f32).reshape(16, 128).T),
        "a16": np.ascontiguousarray(np.asarray(ins["L0_attn_A"], f32)[:, None]),
        "bmT": np.ascontiguousarray(np.asarray(ins["L0_attn_Bm"], f32).T),
        "cmT": np.ascontiguousarray(np.asarray(ins["L0_attn_Cm"], f32).T),
        "bm_rows": np.ascontiguousarray(np.asarray(ins["L0_attn_Bm"], f32).T),
        "cm_rows": np.ascontiguousarray(np.asarray(ins["L0_attn_Cm"], f32).T),
        "out_wT": np.ascontiguousarray(np.asarray(ins["L0_attn_out_w"], f32).T),
        "out_b": np.ascontiguousarray(np.asarray(ins["L0_attn_out_b"], f32)[None, :]),
        "wgT": np.ascontiguousarray(np.asarray(ins["L0_ffn_w_gate"], f32).T),
        "wuT": np.ascontiguousarray(np.asarray(ins["L0_ffn_w_up"], f32).T),
        "wdT": np.ascontiguousarray(np.asarray(ins["L0_ffn_w_down"], f32).T),
        "ident_bf": np.eye(128, dtype=BF),
        "ident_f": np.eye(128, dtype=np.float32),
        "c_rgts": (np.arange(128)[:, None] > np.arange(128)[None, :]).astype(BF),
        "c_rlet": (np.arange(128)[:, None] <= np.arange(128)[None, :]).astype(BF),
        "c_mnegt": np.where(np.arange(128)[None, :] < np.arange(128)[:, None],
                            np.float32(-1e30), np.float32(0.0)).astype(BF),
    }
    zero = np.zeros((L, DM), f32)
    in_maps = []
    for core in range(8):
        b, h = core // 2, core % 2
        lo = h * Th
        m = dict(shared)
        m["x_in"] = np.ascontiguousarray(x[b, lo:lo + Th])
        m["xl"] = np.ascontiguousarray(x[b, lo - L:lo]) if h == 1 else zero
        m["xr"] = np.ascontiguousarray(x[b, Th:Th + L]) if h == 0 else zero
        sel = np.zeros((128, NS), f32)
        if h == 1:
            prev = core - 1
            for n in range(NS):
                sel[prev * NS + n, n] = 1.0
        m["selmat"] = sel
        in_maps.append(m)
    return in_maps


def kernel(**ins):
    nch = 32
    in_maps = _host_inputs(ins, nch)
    nc = build_program(nch)
    res = run_bass_kernel_spmd(nc, in_maps, core_ids=list(range(8)))
    x = np.asarray(ins["L0_attn_x"])
    Bsz, S, _ = x.shape
    Th = S // 2
    out = np.empty((Bsz, S, DM), np.float32)
    for core in range(8):
        b, h = core // 2, core % 2
        out[b, h * Th:(h + 1) * Th] = res.results[core]["out"]
    return out
